# version 21
# baseline (speedup 1.0000x reference)
import contextlib
import numpy as np
import concourse.bass as bass
import concourse.mybir as mybir
from concourse.bass_utils import run_bass_kernel_spmd
from concourse import library_config

F32 = mybir.dt.float32
BF16 = mybir.dt.bfloat16
FP8 = mybir.dt.float8e4
AF = mybir.ActivationFunctionType
ALU = mybir.AluOpType
AX = mybir.AxisListType

TRUNC = 10 ** 9
SUBT = 10 ** 9


class CFG:
    def __init__(self, S=2, T=2048, C=1024, F=4096, TC=512):
        self.S, self.T, self.C, self.F, self.TC = S, T, C, F, TC
        self.R = 64
        self.H = C // 64
        self.CC = C // 128
        self.FC = F // 128
        self.NCH = T // TC
        self.TT = TC // 128
        self.PAIRS = self.H // 2
        self.HF = TC // 2
        self.HT = self.HF // 128
        self.NH = max(1, C // 512)
        self.W5 = min(C, 512)
        self.GT = T // 128


class Prog:

    def __init__(self):
        self.ops = []
        self.cnt = {}

    def c(self, sem):
        return self.cnt.get(sem, 0)

    def op(self, eng, fn, waits=(), inc=None, incn=1):
        w = [(s, v) for (s, v) in waits if v > 0]
        if inc is not None:
            self.cnt[inc] = self.cnt.get(inc, 0) + incn
        self.ops.append((eng, fn, w, (inc, incn) if inc else None))
        return self.cnt.get(inc, 0) if inc else None

    def run(self, engname, e, sems):
        lastw = {}
        for (en, fn, waits, inc) in self.ops:
            if en != engname:
                continue
            for (s, v) in waits:
                if lastw.get(s, 0) < v:
                    e.wait_ge(sems[s], v)
                    lastw[s] = v
            inst = fn(e)
            if inc is not None:
                inst.then_inc(sems[inc[0]], inc[1])


SEMS = ["pe", "act", "dve", "pool",
        "d_pre", "d_lat0", "d_lat1", "d_hm",
        "d_xr0", "d_xr1", "d_st0", "d_st1",
        "d_wn0", "d_wn1", "d_wn2", "d_wn3",
        "d_wt0", "d_wt1", "d_wt2", "d_wt3",
        "g_xt0", "g_xt1", "g_sw10", "g_sw11", "g_sw20", "g_sw21",
        "g_wl", "g_swW0", "g_swW1"]


def build_program(cfg: CFG):
    nc = bass.Bass("TRN2", target_bir_lowering=False, debug=False)
    c = cfg
    H, R, C, T, TC, CC, FC, TT = c.H, c.R, c.C, c.T, c.TC, c.CC, c.FC, c.TT
    HW = TC // 2

    def din(name, shape, dt=F32):
        return nc.dram_tensor(name, shape, dt, kind="ExternalInput").ap()

    hsT = din("hsT", [c.S, C, T])
    hs = din("hs", [c.S, T, C])
    bwT = din("bwT", [C, R])
    coreA = din("coreA", [R, R])
    coreB = din("coreB", [R, R])
    hrr = din("hrr", [R, H * R])
    vwT = din("vwT", [C, C])
    owT = din("owT", [C, C])
    vbr = din("vbr", [1, C])
    obr = din("obr", [1, C])
    w1n = din("w1n", [c.F, C])
    w1t = din("w1t", [C, c.F])
    b1c_d = din("b1c", [128, FC])
    w2n = din("w2n", [C, c.F])
    w2t = din("w2t", [c.F, C])
    b2c_d = din("b2c", [128, CC])
    out_d = nc.dram_tensor("out", [c.S, T, C], F32, kind="ExternalOutput").ap()
    sw1scr = nc.dram_tensor("sw1scr", [C, c.F], FP8, kind="Internal").ap()
    sw2scr = nc.dram_tensor("sw2scr", [c.F, C], FP8, kind="Internal").ap()

    es = contextlib.ExitStack()

    def sb(name, shape, dt):
        return es.enter_context(nc.sbuf_tensor(name, shape, dt))

    def psum(name, shape, dt=F32):
        return es.enter_context(nc.psum_tensor(name, shape, dt))

    vwTs = sb("vwTs", [128, CC, C], BF16)
    owTs = sb("owTs", [128, CC, C], BF16)
    bwTs = sb("bwTs", [128, CC, R], BF16)
    hm2 = sb("hm2", [128, H * 64], BF16)
    vb_b = sb("vb_b", [128, C], BF16)
    ob_b = sb("ob_b", [128, C], F32)
    al1c = sb("al1c", [128, FC], F32)
    b1c = sb("b1c_s", [128, FC], F32)
    al2c = sb("al2c", [128, CC], F32)
    b2c = sb("b2c_s", [128, CC], F32)
    masks = sb("masks", [128, TT, TC], BF16)
    id16 = sb("id16", [128, 128], BF16)
    id32 = sb("id32", [128, 128], F32)
    ones64 = sb("ones64", [64, 1], BF16)
    consts = sb("consts", [128, 8], F32)

    f2r = sb("f2r", [1, H], F32)
    sqfr = sb("sqfr", [1, H], F32)
    halfr = sb("halfr", [1, H], F32)
    smax = sb("smax", [1, 1], F32)
    khat = sb("khat", [1, 1], F32)
    negm1 = sb("negm1", [1, 1], F32)
    negmc = sb("negmc", [128, 1], F32)


    latd = sb("latd", [128, T], BF16)
    Vt = sb("Vt", [128, c.GT * H * 65], BF16)
    n2row = sb("n2row", [1, TC], F32)
    xT = [sb(f"xT{i}", [128, CC, HW], BF16) for i in range(2)]
    xres = [sb(f"xres{i}", [128, C], F32) for i in range(2)]
    qp = sb("qp", [128, TC], BF16)
    Pp = [sb(f"Pp{i}", [128, 2 * TC], BF16) for i in range(2)]
    Praw = sb("Praw", [128, 2 * TC], BF16)
    lbx = sb("lbx", [128, H * R], F32)
    lrec = sb("lrec", [128, 2 * TC], F32)
    lbf = lrec[0:64, :]
    l16 = lrec[64:128, :].bitcast(BF16)[:, 0:2 * TC]
    onesb = sb("onesb", [128, 128], BF16)
    negm16 = sb("negm16", [1, 1], BF16)
    OTf = sb("OTf", [128, c.PAIRS, TC], BF16)
    ubuf = [sb(f"ubuf{i}", [128, C], F32) for i in range(2)]
    iosc = ubuf[0][:, 0:max(TC, 128)]

    outb = [sb(f"outb{i}", [128, C], F32) for i in range(2)]
    if C >= 768:
        presc = outb[0][0:64, 0:768]
    else:
        presc = sb("presc", [64, 768], F32)[0:64, :]

    mvb = sb("mvb", [128, 4], F32)
    sbb = sb("sbb", [128, 8], F32)
    bnsc = sb("bnsc", [128, 12 * ((C + 511) // 512)], F32)
    x2 = sb("x2", [128, TT, C], F32)
    x2b = sb("x2b", [128, C], BF16)
    sq64 = x2b[0:64, 0:C]

    x2T = sb("x2T", [128, CC, TC], BF16)
    hbuf = sb("hbuf", [128, FC, c.HF], BF16)
    sw1b = [sb(f"sw1b{i}", [128, CC, 512], FP8) for i in range(2)]
    sw2b = [sb(f"sw2b{i}", [128, 2, C], FP8) for i in range(2)]
    y2t = sb("y2t", [128, CC, c.HF], F32)[:]
    if FC * c.HF // 2 >= 4 * C and CC * c.HF >= 2 * C:
        _hb32 = hbuf[:].rearrange("p a b -> p (a b)").bitcast(F32)
        _y32 = y2t[:].rearrange("p a b -> p (a b)")
        wna = [_hb32[:, 0:C], _hb32[:, C:2 * C],
               outb[1][:, 0:C], ubuf[1][:, 0:C]]
        wnt = [_hb32[:, 2 * C:3 * C], _hb32[:, 3 * C:4 * C],
               _y32[:, 0:C], _y32[:, C:2 * C]]
    else:
        wna = [sb(f"wna{i}", [128, C], F32)[:] for i in range(4)]
        wnt = [sb(f"wnt{i}", [128, C], F32)[:] for i in range(4)]
    _x2t16 = x2T[:].rearrange("p a b -> p (a b)")
    wsg = [_x2t16[:, C * i:C * (i + 1)] for i in range(2)]

    pSW = psum("pSW", [128, 1024], F32)
    pS = pSW[:, 0:512]
    pS2 = pSW[:, 512:1024]
    pSX = psum("pSX", [128, 1024], F32)
    pS3 = pSX[:, 0:512]
    pS4 = pSX[:, 512:1024]
    pO = psum("pO", [128, 1024], F32)
    pF1 = psum("pF1", [128, 512], F32)
    pF2 = psum("pF2", [128, 512], F32)
    SBK = [(pS, pS2), (pS3, pS4)]
    SBKW = [pSW, pSX]
    _b2 = [pF1, pF2]
    _b2n = ["pF1", "pF2"]
    NP2 = max(1, (CC * c.HF) // 1024)
    DCP = CC // NP2
    y2banks = [(_b2[((dc % DCP) * c.HF) // 512],
                ((dc % DCP) * c.HF) % 512,
                _b2n[((dc % DCP) * c.HF) // 512]) for dc in range(CC)]
    assert DCP * c.HF <= 2 * 512

    sems = {s: es.enter_context(nc.semaphore(s)) for s in SEMS}
    P = Prog()

    bankg = {k: [] for k in
             ["pS", "pS2", "pS3", "pS4", "pOa", "pOb", "pF1", "pF2"]}

    def guard(*names):
        w = []
        for n in names:
            w += bankg[n]
        return w

    def setg(name, sem, val):
        bankg[name] = [(sem, val)]

    lat_evac = [0]
    n2_evac = [0]
    q_evac = [0]
    ot_evac = [0]
    latdup_cnt = [0, 0]
    latdup_order = [0]
    x2t_evac = [0]
    x2b_free = [0]
    u_free = [0, 0]
    u2_free = [0]
    x2_wr_ok = [0]
    otn_done = [0]
    s_done = [("pe", 0), ("pe", 0)]
    x2T_rd_free = [0]
    y2t_free = [0]
    outb_free = [0, 0]
    h_free = [0, 0]
    praw_free = [0]
    st_ = {
        "xt_load": [0, 0], "xt_free": [0, 0],
        "xr_load": [0, 0], "xr_free": [0, 0],
        "st_done": [0, 0],
        "sw1_load": [0, 0], "sw1_free": [0, 0],
        "sw2_load": [0, 0], "sw2_free": [0, 0],
        "pp_free": [0, 0],
        "negm_ready": 0, "x2_ready": 0, "x2T_ready": 0, "h_done": 0,
        "x2T_half": [0, 0],
    }
    pre = {'sw1_blk': {}, 'sw2_blk': {}, 'al1_ready': {}}
    wn_free = [0, 0, 0, 0]
    wt_free = [0, 0, 0, 0]
    sg_free = [0, 0]

    def emit_preamble():
        P.op("gpsimd", lambda e: e.iota(
            iosc[:, 0:128], pattern=[[1, 128]], base=0, channel_multiplier=-1,
            allow_small_or_imprecise_dtypes=True), inc="pool")
        P.op("vector", lambda e: e.tensor_scalar(
            id16[:], iosc[:, 0:128], 0.0, None, op0=ALU.is_equal),
            waits=[("pool", P.c("pool"))], inc="dve")
        P.op("vector", lambda e: e.tensor_scalar(
            id32[:], iosc[:, 0:128], 0.0, None, op0=ALU.is_equal), inc="dve")
        for o in range(TT):
            pv = P.op("gpsimd", lambda e, o=o: e.iota(
                iosc[:, 0:TC], pattern=[[1, TC]], base=-128 * o,
                channel_multiplier=-1, allow_small_or_imprecise_dtypes=True),
                waits=[("dve", P.c("dve"))], inc="pool")
            P.op("vector", lambda e, o=o: e.tensor_scalar(
                masks[:, o, :], iosc[:, 0:TC], 0.0, -60.0, op0=ALU.is_lt,
                op1=ALU.mult), waits=[("pool", pv)], inc="dve")
        P.op("vector", lambda e: e.memset(ones64[:], 1.0), inc="dve")
        P.op("vector", lambda e: e.memset(onesb[:], 1.0), inc="dve")
        P.op("vector", lambda e: e.memset(consts[:, 0:1], -0.5), inc="dve")
        P.op("vector", lambda e: e.memset(halfr[:], 0.5), inc="dve")
        P.op("vector", lambda e: e.memset(khat[:], 0.0), inc="dve")

        for dst, src in [(presc[0:64, 0:R], coreA),
                         (presc[0:64, 64:128], coreB),
                         (ob_b[0:1, :], obr),
                         (b1c[:], b1c_d), (b2c[:], b2c_d),
                         (lbx[0:64, 0:H * R], hrr)]:
            P.op("sync", lambda e, d=dst, s=src: e.dma_start(out=d, in_=s),
                 inc="d_pre", incn=16)
        pre_small = P.c("d_pre")

        for dst, src in [
                (vb_b[0:1, :], vbr),
                (bwTs[:], bwT.rearrange("(cc p) r -> p cc r", p=128)),
                (vwTs[:], vwT.rearrange("(cc p) co -> p cc co", p=128)),
                (owTs[:], owT.rearrange("(cc p) co -> p cc co", p=128))]:
            P.op("gpsimd", lambda e, d=dst, s=src: e.dma_start(out=d, in_=s),
                 inc="g_wl", incn=16)
        pre["wl_done"] = P.c("g_wl")

        for buf in (vb_b, ob_b):
            k = 1
            while k < 128:
                P.op("sync", lambda e, buf=buf, k=k: e.dma_start(
                    out=buf[k:2 * k, :], in_=buf[0:k, :]),
                    waits=[("d_pre", P.c("d_pre")),
                           ("g_wl", pre["wl_done"])], inc="d_pre", incn=16)
                k *= 2
        pre["bcast_done"] = P.c("d_pre")

        P.op("vector", lambda e: e.tensor_add(
            presc[0:64, 128:128 + H * R // 2], lbx[0:64, 0:H * R // 2],
            lbx[0:64, H * R // 2:H * R]),
            waits=[("d_pre", pre_small)], inc="dve")
        width = H * R // 2
        while width > 64:
            w2 = width // 2
            P.op("vector", lambda e, w2=w2, width=width: e.tensor_add(
                presc[0:64, 128:128 + w2], presc[0:64, 128:128 + w2],
                presc[0:64, 128 + w2:128 + width]),
                waits=[("dve", P.c("dve"))], inc="dve")
            width = w2
        P.op("vector", lambda e: e.tensor_add(
            presc[0:64, 704:768], presc[0:64, 0:64], presc[0:64, 64:128]),
            waits=[("dve", P.c("dve"))], inc="dve")
        P.op("vector", lambda e: e.tensor_scalar(
            presc[0:64, 704:768], presc[0:64, 704:768], 0.5 / H / 8.0, None,
            op0=ALU.mult), waits=[("dve", P.c("dve"))], inc="dve")
        P.op("vector", lambda e: e.tensor_scalar(
            presc[0:64, 640:704], presc[0:64, 128:192], -1.0 / H / 8.0, None,
            op0=ALU.mult), waits=[("dve", P.c("dve"))], inc="dve")
        P.op("vector", lambda e: e.tensor_add(
            presc[0:64, 640:704], presc[0:64, 640:704], presc[0:64, 704:768]),
            waits=[("dve", P.c("dve"))], inc="dve")
        for h in range(H):
            P.op("vector", lambda e, h=h: e.scalar_tensor_tensor(
                hm2[0:64, 64 * h:64 * h + 64], lbx[0:64, 64 * h:64 * h + 64],
                0.125, presc[0:64, 640:704], op0=ALU.mult, op1=ALU.add),
                waits=[("dve", P.c("dve"))], inc="dve")
        hm_low = P.c("dve")
        P.op("sync", lambda e: e.dma_start(out=hm2[64:128, :], in_=hm2[0:64, :]),
             waits=[("dve", hm_low)], inc="d_hm", incn=16)
        pre["hm_dup"] = P.c("d_hm")

        P.op("scalar", lambda e: e.activation(
            sq64[0:64, 0:H * 64], hm2[0:64, :], AF.Square),
            waits=[("dve", hm_low)], inc="act")
        sqd = P.c("act")
        nhalf = (H * 64 + 511) // 512
        for i in range(nhalf):
            wdt = min(512, H * 64 - 512 * i)
            P.op("tensor", lambda e, i=i, wdt=wdt: e.matmul(
                pO[0:1, 512 * i:512 * i + wdt], ones64[:],
                sq64[0:64, 512 * i:512 * i + wdt], start=True, stop=True),
                waits=[("act", sqd), ("dve", P.c("dve"))],
                inc="pe" if i == nhalf - 1 else None)
        P.op("vector", lambda e: e.tensor_reduce(
            f2r[:].unsqueeze(2), pO[0:1, 0:H * 64].rearrange(
                "p (h r) -> p h r", h=H), axis=AX.X, op=ALU.add),
            waits=[("pe", P.c("pe"))], inc="dve")
        P.op("gpsimd", lambda e: e.tensor_tensor(
            sqfr[:], f2r[:], halfr[:], op=ALU.pow),
            waits=[("dve", P.c("dve"))], inc="pool")
        P.op("vector", lambda e: e.tensor_reduce(
            smax[:], sqfr[:], axis=AX.X, op=ALU.max),
            waits=[("pool", P.c("pool"))], inc="dve")
        setg("pOa", "dve", P.c("dve"))

        return None

    def gen_preambleB():
        def sign_stage(src_slice, dst_slice, idx):
            par = idx % 4
            sem = f"d_wt{par}"
            wsem = f"g_swW{idx % 2}"
            fw = [("act", wt_free[par])] if wt_free[par] else []
            P.op("scalar", lambda e, s=src_slice, par=par: e.dma_start(
                out=wnt[par], in_=s), waits=fw, inc=sem, incn=16)
            P.op("scalar", lambda e, par=par, sp=idx % 2: e.sign(
                wsg[sp], wnt[par]),
                 waits=[(sem, P.c(sem)), (wsem, sg_free[idx % 2])], inc="act")
            wt_free[par] = P.c("act")
            P.op("gpsimd", lambda e, d=dst_slice, sp=idx % 2: e.dma_start(
                out=d, in_=wsg[sp]), waits=[("act", P.c("act"))],
                inc=wsem, incn=16)
            sg_free[idx % 2] = P.c(wsem)

        idx = 0
        sg1 = [(a, b) for b in range(c.F // C) for a in range(CC)]
        wn_counts = {}
        wt_counts = {}

        def load_wna1(i):
            par = i % 4
            sem = f"d_wn{par}"
            fw = [("dve", wn_free[par])] if wn_free[par] else []
            P.op("sync", lambda e, i=i, par=par: e.dma_start(
                out=wna[par], in_=w1n[128 * i:128 * (i + 1), :]),
                waits=fw, inc=sem, incn=16)
            wn_counts[i] = P.c(sem)

        def load_wnt1(t):
            a, b = sg1[t]
            par = t % 4
            sem = f"d_wt{par}"
            fw = [("act", wt_free[par])] if wt_free[par] else []
            P.op("sync", lambda e, a=a, b=b, par=par: e.dma_start(
                out=wnt[par], in_=w1t[128 * a:128 * (a + 1),
                                      C * b:C * (b + 1)]),
                waits=fw, inc=sem, incn=16)
            wt_counts[t] = P.c(sem)

        LOOK = 3
        for i in range(min(LOOK, FC)):
            load_wna1(i)
            load_wnt1(i)
        for i in range(FC):
            if i + LOOK < FC:
                load_wna1(i + LOOK)
                load_wnt1(i + LOOK)
            par = i % 4
            P.op("vector", lambda e, i=i, par=par: e.tensor_reduce(
                al1c[:, i:i + 1], wna[par], axis=AX.X, op=ALU.add,
                apply_absolute_value=True),
                waits=[(f"d_wn{par}", wn_counts[i])], inc="dve")
            P.op("vector", lambda e, i=i: e.tensor_scalar(
                al1c[:, i:i + 1], al1c[:, i:i + 1], 1.0 / C, None,
                op0=ALU.mult), waits=[("dve", P.c("dve"))], inc="dve")
            wn_free[par] = P.c("dve")
            pre["al1_ready"][i] = P.c("dve")
            if i < len(sg1):
                a, b = sg1[i]
                wsem = f"g_swW{i % 2}"
                P.op("scalar", lambda e, par=par, sp=i % 2: e.sign(
                    wsg[sp], wnt[par]),
                    waits=[(f"d_wt{par}", wt_counts[i]),
                           (wsem, sg_free[i % 2])], inc="act")
                wt_free[par] = P.c("act")
                P.op("gpsimd", lambda e, a=a, b=b, sp=i % 2: e.dma_start(
                    out=sw1scr[128 * a:128 * (a + 1), C * b:C * (b + 1)],
                    in_=wsg[sp]), waits=[("act", P.c("act"))],
                    inc=wsem, incn=16)
                sg_free[i % 2] = P.c(wsem)
                if (i + 1) % CC == 0:
                    pre["sw1_blk"][(i + 1) // CC - 1] = (
                        P.c("g_swW0"), P.c("g_swW1"))
                idx += 1
            yield
        pre["sw1_ready"] = (P.c("g_swW0"), P.c("g_swW1"))
        for b in range(c.F // C):
            if b not in pre["sw1_blk"]:
                pre["sw1_blk"][b] = pre["sw1_ready"]

        wn2c = {}
        wt2c = {}

        def load_wna2(j):
            i, b = divmod(j, c.F // C)
            par = j % 4
            sem = f"d_wn{par}"
            P.op("sync", lambda e, i=i, b=b, par=par: e.dma_start(
                out=wna[par], in_=w2n[128 * i:128 * (i + 1),
                                      C * b:C * (b + 1)]),
                waits=[("dve", wn_free[par])], inc=sem, incn=16)
            wn2c[j] = P.c(sem)

        def load_wnt2(t):
            par = t % 4
            sem = f"d_wt{par}"
            fw = [("act", wt_free[par])] if wt_free[par] else []
            P.op("sync", lambda e, t=t, par=par: e.dma_start(
                out=wnt[par], in_=w2t[128 * t:128 * (t + 1), :]),
                waits=fw, inc=sem, incn=16)
            wt2c[t] = P.c(sem)

        NW2 = CC * (c.F // C)
        for j in range(min(3, NW2)):
            load_wna2(j)
            load_wnt2(j)
        for j in range(NW2):
            if j + 3 < NW2:
                load_wna2(j + 3)
                load_wnt2(j + 3)
            i, b = divmod(j, c.F // C)
            par = j % 4
            if b == 0:
                P.op("vector", lambda e, i=i, par=par: e.tensor_reduce(
                    al2c[:, i:i + 1], wna[par], axis=AX.X, op=ALU.add,
                    apply_absolute_value=True),
                    waits=[(f"d_wn{par}", wn2c[j])], inc="dve")
            else:
                P.op("vector", lambda e, par=par: e.tensor_reduce(
                    consts[:, 1:2], wna[par], axis=AX.X, op=ALU.add,
                    apply_absolute_value=True),
                    waits=[(f"d_wn{par}", wn2c[j]), ("dve", P.c("dve"))],
                    inc="dve")
                P.op("vector", lambda e, i=i: e.tensor_add(
                    al2c[:, i:i + 1], al2c[:, i:i + 1], consts[:, 1:2]),
                    waits=[("dve", P.c("dve"))], inc="dve")
            wn_free[par] = P.c("dve")
            if j < FC:
                wsem = f"g_swW{j % 2}"
                P.op("scalar", lambda e, par=par, sp=j % 2: e.sign(
                    wsg[sp], wnt[par]),
                    waits=[(f"d_wt{par}", wt2c[j]),
                           (wsem, sg_free[j % 2])], inc="act")
                wt_free[par] = P.c("act")
                P.op("gpsimd", lambda e, t=j, sp=j % 2: e.dma_start(
                    out=sw2scr[128 * t:128 * (t + 1), :],
                    in_=wsg[sp]), waits=[("act", P.c("act"))],
                    inc=wsem, incn=16)
                sg_free[j % 2] = P.c(wsem)
                if (j + 1) % 2 == 0:
                    pre["sw2_blk"][(j + 1) // 2 - 1] = (
                        P.c("g_swW0"), P.c("g_swW1"))
                idx += 1
            yield
        P.op("vector", lambda e: e.tensor_scalar(
            al2c[:], al2c[:], 1.0 / c.F, None, op0=ALU.mult),
            waits=[("dve", P.c("dve"))], inc="dve")
        pre["sw2_ready"] = (P.c("g_swW0"), P.c("g_swW1"))
        pre["al2_ready"] = P.c("dve")
        pre["wn_done2"] = P.c("d_wn2")
        pre["wn_done3"] = P.c("d_wn3")
        for b in range(FC // 2):
            if b not in pre["sw2_blk"]:
                pre["sw2_blk"][b] = pre["sw2_ready"]

    def emit_xt_load(s, ci, half):
        par = (ci * 2 + half) % 2
        off = ci * TC + half * HW
        sem = f"g_xt{par}"
        waits = [("pe", st_["xt_free"][par])] if st_["xt_load"][par] else []
        src = hsT[s, :, off:off + HW].rearrange("(cc p) t -> p cc t", p=128)
        P.op("gpsimd", lambda e, par=par, src=src: e.dma_start(
            out=xT[par][:], in_=src), waits=waits, inc=sem, incn=16)
        st_["xt_load"][par] = P.c(sem)

    def emit_latent_v(s, ci, half):
        par = (ci * 2 + half) % 2
        off = ci * TC + half * HW
        sem = f"g_xt{par}"
        first_w = [(sem, st_["xt_load"][par]), ("g_wl", pre["wl_done"]),
                   ("dve", lat_evac[0])] + guard("pS")
        for k in range(CC):
            P.op("tensor", lambda e, k=k, par=par: e.matmul(
                pS[0:64, 0:HW], bwTs[:, k, :], xT[par][:, k, :],
                start=(k == 0), stop=(k == CC - 1)),
                waits=first_w if k == 0 else [],
                inc="pe" if k == CC - 1 else None)
        lat_mm = P.c("pe")
        if SUBT < 2:
            return
        P.op("vector", lambda e, off=off: e.tensor_copy(
            latd[0:64, off:off + HW], pS[0:64, 0:HW]),
            waits=[("pe", lat_mm)], inc="dve")
        lat_evac[0] = P.c("dve")
        setg("pS", "dve", lat_evac[0])
        if SUBT < 3:
            return
        P.op("scalar", lambda e: e.activation(
            sq64[0:64, 0:HW], pS[0:64, 0:HW], AF.Square),
            waits=[("pe", lat_mm), ("dve", lat_evac[0])], inc="act")
        setg("pS", "act", P.c("act"))
        P.op("tensor", lambda e: e.matmul(
            pS2[0:1, 0:HW], ones64[:], sq64[0:64, 0:HW], start=True,
            stop=True), waits=[("act", P.c("act")), ("dve", n2_evac[0])]
            + guard("pS2"), inc="pe")
        P.op("vector", lambda e, half=half: e.tensor_copy(
            n2row[0:1, half * HW:(half + 1) * HW], pS2[0:1, 0:HW]),
            waits=[("pe", P.c("pe"))], inc="dve")
        n2_evac[0] = P.c("dve")
        setg("pS2", "dve", n2_evac[0])
        if SUBT < 4:
            return
        dsem = f"d_lat{par}"
        P.op("sync", lambda e, off=off: e.dma_start(
            out=latd[64:128, off:off + HW], in_=latd[0:64, off:off + HW]),
            waits=[("dve", lat_evac[0]), ("pe", latdup_order[0])],
            inc=dsem, incn=16)
        latdup_cnt[par] = P.c(dsem)
        if SUBT < 5:
            return
        vv = Vt[:].rearrange("p (g h v) -> p g h v", g=c.GT, h=H)
        for tt in range(HW // 128):
            gt = off // 128 + tt
            for coh in range(c.NH):
                pob = "pOa" if coh % 2 == 0 else "pOb"
                for k in range(CC):
                    P.op("tensor", lambda e, tt=tt, coh=coh, k=k, par=par: e.matmul(
                        pO[0:128, 512 * (coh % 2):512 * (coh % 2) + c.W5],
                        xT[par][:, k, 128 * tt:128 * tt + 128],
                        vwTs[:, k, c.W5 * coh:c.W5 * (coh + 1)],
                        start=(k == 0), stop=(k == CC - 1)),
                        waits=(guard(pob) + [("d_pre", pre["bcast_done"])]
                               if k == 0 else []),
                        inc="pe" if k == CC - 1 else None)
                vm = P.c("pe")
                hpg = c.W5 // 64
                P.op("vector", lambda e, gt=gt, coh=coh, vv=vv, hpg=hpg: e.tensor_add(
                    vv[:, gt, coh * hpg:(coh + 1) * hpg, 0:64],
                    pO[0:128, 512 * (coh % 2):512 * (coh % 2) + c.W5]
                    .rearrange("p (h v) -> p h v", v=64),
                    vb_b[:, c.W5 * coh:c.W5 * (coh + 1)]
                    .rearrange("p (h v) -> p h v", v=64)),
                    waits=[("pe", vm)], inc="dve")
                setg(pob, "dve", P.c("dve"))
        st_["xt_free"][par] = P.c("pe")

    def emit_negm():
        P.op("vector", lambda e: e.tensor_reduce(
            consts[0:1, 2:3], n2row[:], axis=AX.X, op=ALU.max),
            waits=[("dve", n2_evac[0]), ("dve", st_["negm_ready"])],
            inc="dve")
        P.op("vector", lambda e: e.tensor_tensor(
            khat[:], khat[:], consts[0:1, 2:3], op=ALU.max),
            waits=[("dve", P.c("dve"))], inc="dve")
        P.op("vector", lambda e: e.tensor_scalar(
            negm1[:], khat[:], smax[0:1, 0:1], -1.02, op0=ALU.mult,
            op1=ALU.mult), waits=[("dve", P.c("dve"))], inc="dve")
        P.op("vector", lambda e: e.tensor_copy(negm16[:], negm1[:]),
             waits=[("dve", P.c("dve"))], inc="dve")
        P.op("tensor", lambda e: e.matmul(
            pO[0:128, 0:1], onesb[0:1, 0:128], negm16[:], start=True,
            stop=True), waits=[("dve", P.c("dve"))] + guard("pOa"), inc="pe")
        P.op("vector", lambda e: e.tensor_copy(negmc[:], pO[0:128, 0:1]),
             waits=[("pe", P.c("pe"))], inc="dve")
        setg("pOa", "dve", P.c("dve"))
        st_["negm_ready"] = P.c("dve")

    def gen_attn_pair(s, ci, p):
        JT = TT * (ci + 1)
        coff = ci * TC
        h0, h1 = 2 * p, 2 * p + 1
        ldw = [("d_lat0", latdup_cnt[0]), ("d_lat1", latdup_cnt[1]),
               ("d_hm", pre["hm_dup"])]
        P.op("tensor", lambda e, h0=h0, coff=coff: e.matmul(
            pO[0:64, 0:TC], hm2[0:64, 64 * h0:64 * h0 + 64],
            latd[0:64, coff:coff + TC], start=True, stop=True,
            tile_position=(0, 0)),
            waits=ldw + [("act", q_evac[0])] + guard("pOa", "pOb"), inc="pe")
        P.op("tensor", lambda e, h1=h1, coff=coff: e.matmul(
            pO[64:128, 0:TC], hm2[64:128, 64 * h1:64 * h1 + 64],
            latd[64:128, coff:coff + TC], start=True, stop=True,
            tile_position=(64, 64)), inc="pe")
        qm = P.c("pe")
        latdup_order[0] = qm
        P.op("scalar", lambda e: e.activation(qp[:], pO[0:128, 0:TC], AF.Copy),
             waits=[("pe", qm)], inc="act")
        q_evac[0] = P.c("act")
        setg("pOa", "act", q_evac[0])

        def joff(j):
            return 128 * (j - TT * ci) if j >= TT * ci else 0

        def emit_S(j):
            par = j % 2
            bA, bB = SBK[par]
            gA = "pS" if par == 0 else "pS3"
            gB = "pS2" if par == 0 else "pS4"
            off = joff(j)
            P.op("tensor", lambda e, j=j, bA=bA, off=off: e.matmul(
                bA[0:128, off:TC], latd[0:64, 128 * j:128 * j + 128],
                qp[0:64, off:TC], start=True, stop=True,
                tile_position=(0, 0)),
                waits=[("act", q_evac[0])] + guard(gA, gB), inc="pe")
            P.op("tensor", lambda e, j=j, bB=bB, off=off: e.matmul(
                bB[0:128, off:TC], latd[64:128, 128 * j:128 * j + 128],
                qp[64:128, off:TC], start=True, stop=True,
                tile_position=(64, 0)),
                inc="pe")
            sm = P.c("pe")
            if j >= TT * ci:
                o = j - TT * ci
                P.op("vector", lambda e, o=o, bA=bA, off=off:
                     e.scalar_tensor_tensor(
                         bA[:, off:TC], bA[:, off:TC], 1.0,
                         masks[:, o, off:TC],
                         op0=ALU.mult, op1=ALU.add), waits=[("pe", sm)],
                     inc="dve")
                P.op("vector", lambda e, o=o, bB=bB, off=off:
                     e.scalar_tensor_tensor(
                         bB[:, off:TC], bB[:, off:TC], 1.0,
                         masks[:, o, off:TC],
                         op0=ALU.mult, op1=ALU.add), inc="dve")
                s_done[j % 2] = ("dve", P.c("dve"))
            else:
                s_done[j % 2] = ("pe", sm)

        emit_S(0)
        if JT > 1:
            emit_S(1)
        for j in range(JT):
            par = j % 2
            bA, bB = SBK[par]
            gA = "pS" if par == 0 else "pS3"
            gB = "pS2" if par == 0 else "pS4"
            dst = Pp[par]
            off = joff(j)
            ew = [s_done[par], ("dve", st_["negm_ready"]),
                  ("pe", st_["pp_free"][par])]
            if TC == 512 and off == 0:
                P.op("scalar", lambda e, dst=dst, par=par: e.activation(
                    dst[:, 0:2 * TC], SBKW[par][:, 0:1024], AF.Exp,
                    bias=negmc[:, 0:1]), waits=ew, inc="act")
            else:
                P.op("scalar", lambda e, bA=bA, dst=dst, off=off: e.activation(
                    dst[:, off:TC], bA[:, off:TC], AF.Exp,
                    bias=negmc[:, 0:1]), waits=ew, inc="act")
                P.op("scalar", lambda e, bB=bB, dst=dst, off=off: e.activation(
                    dst[:, TC + off:2 * TC], bB[:, off:TC], AF.Exp,
                    bias=negmc[:, 0:1]), inc="act")
            ed = P.c("act")
            setg(gA, "act", ed)
            setg(gB, "act", ed)
            if j + 2 < JT:
                emit_S(j + 2)
            vv = Vt[:].rearrange("p (g h v) -> p g h v", g=c.GT, h=H)
            P.op("tensor", lambda e, j=j, par=par, vv=vv, h0=h0, JT=JT,
                 off=off: e.matmul(
                     pO[0:65, off:TC], vv[:, j, h0, :], Pp[par][:, off:TC],
                     start=(j == 0), stop=(j == JT - 1)),
                 waits=[("act", ed)], inc="pe")
            P.op("tensor", lambda e, j=j, par=par, vv=vv, h1=h1, JT=JT,
                 off=off: e.matmul(
                     pO[0:65, 512 + off:512 + TC], vv[:, j, h1, :],
                     Pp[par][:, TC + off:2 * TC],
                     start=(j == 0), stop=(j == JT - 1)), inc="pe")
            st_["pp_free"][par] = P.c("pe")
            yield
        pvd = P.c("pe")
        P.op("vector", lambda e: e.tensor_copy(
            l16[0:1, :].rearrange("p (b t) -> p b t", b=2),
            pO[64:65, :].rearrange("p (b t) -> p b t", b=2)[:, :, 0:TC]),
            waits=[("pe", pvd), ("dve", praw_free[0])], inc="dve")
        lc = P.c("dve")
        P.op("vector", lambda e: e.tensor_copy(
            Praw[0:64, 0:TC], pO[0:64, 0:TC]), inc="dve")
        P.op("vector", lambda e: e.tensor_copy(
            Praw[64:128, 0:TC], pO[0:64, 512:512 + TC]), inc="dve")
        ot_evac[0] = P.c("dve")
        setg("pOa", "dve", ot_evac[0])
        setg("pOb", "dve", ot_evac[0])
        P.op("tensor", lambda e: e.matmul(
            pS[0:64, 0:TC], onesb[64:65, 0:64], l16[0:1, 0:TC],
            start=True, stop=True, tile_position=(64, 0)),
            waits=[("dve", lc)] + guard("pS", "pS2"), inc="pe")
        P.op("tensor", lambda e: e.matmul(
            pSW[64:128, 0:TC], onesb[64:65, 0:64], l16[0:1, TC:2 * TC],
            start=True, stop=True, tile_position=(64, 64)), inc="pe")
        lb = P.c("pe")
        yield
        P.op("vector", lambda e: e.tensor_copy(
            lrec[:, 0:TC], pSW[0:128, 0:TC]), waits=[("pe", lb)], inc="dve")
        lbe = P.c("dve")
        setg("pS", "dve", lbe)
        setg("pS2", "dve", lbe)
        P.op("vector", lambda e: e.reciprocal(
            lrec[:, TC:2 * TC], lrec[:, 0:TC]),
            waits=[("dve", lbe)], inc="dve")
        lbc = P.c("dve")
        P.op("vector", lambda e, p=p: e.tensor_mul(
            OTf[:, p, :], Praw[:, 0:TC], lrec[:, TC:2 * TC]),
            waits=[("dve", lbc)], inc="dve")
        praw_free[0] = P.c("dve")
        otn_done[0] = P.c("dve")
        yield

    def emit_xres_load(s, ci, tt):
        par = tt % 2
        sem = f"d_xr{par}"
        waits = [("dve", st_["xr_free"][par])] if st_["xr_load"][par] else []
        tok = ci * TC + tt * 128
        P.op("sync", lambda e, s=s, tok=tok, par=par: e.dma_start(
            out=xres[par][:], in_=hs[s, tok:tok + 128, :]),
            waits=waits, inc=sem, incn=16)
        st_["xr_load"][par] = P.c(sem)

    def emit_ln(u_parts, dst_parts, w_pre, sset=0):
        nst = len(u_parts)
        bo = 12 * ((C + 511) // 512) // 2 * sset
        mo, so = 2 * sset, 4 * sset
        for i, ua in enumerate(u_parts):
            P.op("vector", lambda e, ua=ua, i=i: e.bn_stats(
                bnsc[:, bo + 6 * i:bo + 6 * (i + 1)], ua),
                waits=(list(w_pre) if i == 0 else []), inc="dve")
        P.op("vector", lambda e, nst=nst: e.bn_aggr(
            mvb[:, mo:mo + 2], bnsc[:, bo:bo + 6 * nst]),
            waits=[("dve", P.c("dve"))], inc="dve")
        P.op("vector", lambda e: e.tensor_scalar(
            sbb[:, so:so + 1], mvb[:, mo + 1:mo + 2], 1e-5, None, op0=ALU.add),
            waits=[("dve", P.c("dve"))], inc="dve")
        P.op("gpsimd", lambda e: e.tensor_tensor(
            sbb[:, so + 1:so + 2], sbb[:, so:so + 1], consts[:, 0:1],
            op=ALU.pow), waits=[("dve", P.c("dve"))], inc="pool")
        pw = P.c("pool")
        for i, (ua, da) in enumerate(zip(u_parts, dst_parts)):
            P.op("vector", lambda e, ua=ua, da=da: e.tensor_scalar(
                da, ua, mvb[:, mo:mo + 1], sbb[:, so + 1:so + 2],
                op0=ALU.subtract, op1=ALU.mult),
                waits=([("pool", pw)] if i == 0 else []), inc="dve")
        return P.c("dve")

    def emit_oproj_ln1(s, ci, tiles=None):
        for tt in (range(TT) if tiles is None else tiles):
            par = tt % 2
            for coh in range(c.NH):
                dstp = pS if coh % 2 == 0 else pS2
                gname = "pS" if coh % 2 == 0 else "pS2"
                for p in range(c.PAIRS):
                    P.op("tensor", lambda e, p=p, tt=tt, coh=coh, dstp=dstp:
                         e.matmul(
                             dstp[:, 0:c.W5],
                             OTf[:, p, 128 * tt:128 * tt + 128],
                             owTs[:, p, c.W5 * coh:c.W5 * (coh + 1)],
                             start=(p == 0), stop=(p == c.PAIRS - 1)),
                         waits=([("dve", otn_done[0])] + guard(gname)
                                if p == 0 else []),
                         inc="pe" if p == c.PAIRS - 1 else None)
            ym = P.c("pe")
            ub = ubuf[par]
            for coh in range(c.NH):
                srcp = pS if coh % 2 == 0 else pS2
                gname = "pS" if coh % 2 == 0 else "pS2"
                P.op("vector", lambda e, ub=ub, coh=coh, srcp=srcp: e.tensor_add(
                    ub[:, c.W5 * coh:c.W5 * (coh + 1)], srcp[:, 0:c.W5],
                    ob_b[:, c.W5 * coh:c.W5 * (coh + 1)]),
                    waits=[("pe", ym), (f"d_xr{par}", st_["xr_load"][par]),
                           ("dve", u_free[par]),
                           ("d_wn3", pre.get("wn_done3", 0) if par else 0)],
                    inc="dve")
                P.op("vector", lambda e, ub=ub, coh=coh, par=par: e.tensor_add(
                    ub[:, c.W5 * coh:c.W5 * (coh + 1)],
                    ub[:, c.W5 * coh:c.W5 * (coh + 1)],
                    xres[par][:, c.W5 * coh:c.W5 * (coh + 1)]),
                    waits=[("dve", P.c("dve"))], inc="dve")
                setg(gname, "dve", P.c("dve"))
            st_["xr_free"][par] = P.c("dve")
            ud = P.c("dve")
            a = emit_ln([ub[:, c.W5 * i:c.W5 * (i + 1)] for i in range(c.NH)],
                        [x2[:, tt, c.W5 * i:c.W5 * (i + 1)]
                         for i in range(c.NH)],
                        [("dve", ud), ("dve", x2_wr_ok[0])], sset=0)
            u_free[par] = a
            if tt + 2 < TT:
                emit_xres_load(s, ci, tt + 2)
            P.op("vector", lambda e, tt=tt: e.tensor_copy(x2b[:], x2[:, tt, :]),
                 waits=[("dve", a), ("pe", x2b_free[0]),
                        ("pe", x2T_rd_free[0])], inc="dve")
            cb = P.c("dve")
            for k in range(CC):
                tb = pF1 if k % 2 == 0 else pF2
                gname = "pF1" if k % 2 == 0 else "pF2"
                P.op("tensor", lambda e, k=k, tb=tb: e.transpose(
                    tb[:, 0:64].bitcast(BF16), x2b[:, 128 * k:128 * k + 128],
                    id16[:]),
                    waits=[("dve", cb)] + guard(gname), inc="pe")
                P.op("vector", lambda e, k=k, tt=tt, tb=tb: e.tensor_copy(
                    x2T[:, k, 128 * tt:128 * tt + 128],
                    tb[:, 0:64].bitcast(BF16)),
                    waits=[("pe", P.c("pe")),
                           ("g_swW0", pre["sw2_ready"][0]),
                           ("g_swW1", pre["sw2_ready"][1])], inc="dve")
                setg(gname, "dve", P.c("dve"))
                x2t_evac[0] = P.c("dve")
            x2b_free[0] = P.c("pe")
            st_["x2T_half"][min(1, tt // c.HT)] = P.c("dve")
        st_["x2_ready"] = P.c("dve")
        st_["x2T_ready"] = P.c("dve")

    def emit_sw1_load(gidx):
        par = gidx % 2
        sem = f"g_sw1{par}"
        blk = gidx % (c.F // 512)
        rdy = pre["sw1_blk"].get((512 * blk) // C, pre.get("sw1_ready", (0, 0)))
        waits = [("g_swW0", rdy[0]), ("g_swW1", rdy[1])]
        if st_["sw1_load"][par]:
            waits.append(("pe", st_["sw1_free"][par]))
        src = sw1scr[:, 512 * blk:512 * blk + 512].rearrange(
            "(cc p) co -> p cc co", p=128)
        P.op("sync", lambda e, par=par, src=src: e.dma_start(
            out=sw1b[par][:], in_=src), waits=waits, inc=sem, incn=16)
        st_["sw1_load"][par] = P.c(sem)
        return par

    def emit_sw2_load(gidx):
        par = gidx % 2
        sem = f"g_sw2{par}"
        blk = gidx % (FC // 2)
        rdy = pre["sw2_blk"].get(blk, pre.get("sw2_ready", (0, 0)))
        waits = [("g_swW0", rdy[0]), ("g_swW1", rdy[1])]
        if st_["sw2_load"][par]:
            waits.append(("pe", st_["sw2_free"][par]))
        src = sw2scr[256 * blk:256 * blk + 256, :].rearrange(
            "(hc p) d -> p hc d", p=128)
        P.op("sync", lambda e, par=par, src=src: e.dma_start(
            out=sw2b[par][:], in_=src), waits=waits, inc=sem, incn=16)
        st_["sw2_load"][par] = P.c(sem)
        return par

    sw1_gidx = [0]
    sw2_gidx = [0]
    q1g = []
    q2g = []

    def issue1g():
        q1g.append(emit_sw1_load(sw1_gidx[0]))
        sw1_gidx[0] += 1

    def issue2g():
        q2g.append(emit_sw2_load(sw2_gidx[0]))
        sw2_gidx[0] += 1

    def gen_ffn1(s, ci, half):
        HF = c.HF
        if not q1g:
            issue1g()
        if True:
            toff = half * HF
            for blk in range(c.F // 512):
                while len(q1g) < 2 and blk + 1 < c.F // 512:
                    issue1g()
                par = q1g.pop(0)
                for cl in range(4):
                    co = 4 * blk + cl
                    bank = pS3 if co % 2 == 0 else pS4
                    gname = "pS3" if co % 2 == 0 else "pS4"
                    for k in range(CC):
                        P.op("tensor", lambda e, par=par, cl=cl, k=k,
                             bank=bank, toff=toff: e.matmul(
                                 bank[:, 0:HF],
                                 sw1b[par][:, k, 128 * cl:128 * cl + 128],
                                 x2T[:, k, toff:toff + HF],
                                 start=(k == 0), stop=(k == CC - 1)),
                             waits=([(f"g_sw1{par}", st_["sw1_load"][par]),
                                     ("dve", st_["x2T_half"][half]),
                                     ("act", h_free[co % 2])] + guard(gname)
                                    if k == 0 else []),
                             inc="pe" if k == CC - 1 else None)
                    hm_ = P.c("pe")
                    x2T_rd_free[0] = hm_
                    gw = [("pe", hm_),
                          ("dve", pre["al1_ready"].get(co, 0))]
                    if co >= FC // 2:
                        gw.append(("pe", y2t_free[0]))
                    P.op("scalar", lambda e, co=co, bank=bank: e.activation(
                        hbuf[:, co, :], bank[:, 0:HF], AF.Gelu,
                        bias=b1c[:, co:co + 1], scale=al1c[:, co:co + 1]),
                        waits=gw, inc="act")
                    h_free[co % 2] = P.c("act")
                    setg(gname, "act", P.c("act"))
                    yield
                st_["sw1_free"][par] = P.c("pe")
        st_["h_done"] = P.c("act")

    def gen_ffn2(s, ci, half):
        HF, HT = c.HF, c.HT
        if not q2g:
            issue2g()
        if True:
            for dpass in range(NP2):
                dlo, dhi = dpass * DCP, (dpass + 1) * DCP
                firstw = ([("act", st_["h_done"])] +
                          guard("pF1", "pF2"))
                for blk in range(FC // 2):
                    while len(q2g) < 2 and (blk + 1 < FC // 2 or
                                            dpass + 1 < NP2):
                        issue2g()
                    par = q2g.pop(0)
                    for hcl in range(2):
                        hc = 2 * blk + hcl
                        for dc in range(dlo, dhi):
                            bank, boff, _gn = y2banks[dc]
                            first_in_bank = (boff == 0)
                            last_in_bank = (boff + HF == 512) or (dc == dhi - 1)
                            stt = (hc == 0) and first_in_bank
                            stp = (hc == FC - 1) and last_in_bank
                            w = []
                            if hcl == 0 and dc == dlo:
                                w = [(f"g_sw2{par}", st_["sw2_load"][par])]
                                if blk == 0:
                                    w += firstw
                            P.op("tensor", lambda e, par=par, hcl=hcl, hc=hc,
                                 dc=dc, bank=bank, boff=boff, stt=stt,
                                 stp=stp: e.matmul(
                                     bank[:, boff:boff + HF],
                                     sw2b[par][:, hcl,
                                               128 * dc:128 * dc + 128],
                                     hbuf[:, hc, :], start=stt, stop=stp),
                                 waits=w, inc="pe" if dc == dhi - 1 else None)
                    st_["sw2_free"][par] = P.c("pe")
                    yield
                y2m = P.c("pe")
                for dc in range(dlo, dhi):
                    bank, boff, _gn = y2banks[dc]
                    P.op("scalar", lambda e, dc=dc, bank=bank, boff=boff:
                         e.activation(
                             y2t[:, dc, :], bank[:, boff:boff + HF],
                             AF.Identity, bias=b2c[:, dc:dc + 1],
                             scale=al2c[:, dc:dc + 1]),
                         waits=[("pe", y2m),
                                ("dve", pre.get("al2_ready", 0))], inc="act")
                y2ev = P.c("act")
                for nm in ("pF1", "pF2"):
                    setg(nm, "act", y2ev)
                yield
            for tt in range(HT):
                gtt = half * HT + tt
                for dc in range(CC):
                    tb = pF1 if dc < 4 else pF2
                    gname = "pF1" if dc < 4 else "pF2"
                    P.op("tensor", lambda e, dc=dc, tt=tt, tb=tb: e.transpose(
                        tb[:, 128 * (dc % 4):128 * (dc % 4) + 128],
                        y2t[:, dc, 128 * tt:128 * tt + 128], id32[:]),
                        waits=([("act", y2ev)] + guard("pF1", "pF2")
                               if dc == 0 else []), inc="pe")
                trm = P.c("pe")
                y2t_free[0] = trm
                par = gtt % 2
                ob2 = outb[par]
                ub = ubuf[par]
                uw = [("pe", trm), ("dve", st_["x2_ready"]),
                      ("dve", u_free[par]),
                      ("d_wn3", pre.get("wn_done3", 0) if par else 0),
                      ("d_wn2", pre.get("wn_done2", 0) if par else 0)]
                for nh in range(c.NH):
                    sb_ = pF1 if nh == 0 else pF2
                    P.op("vector", lambda e, gtt=gtt, nh=nh, sb_=sb_,
                         ub=ub: e.tensor_add(
                             ub[:, c.W5 * nh:c.W5 * (nh + 1)],
                             sb_[:, 0:c.W5],
                             x2[:, gtt, c.W5 * nh:c.W5 * (nh + 1)]),
                         waits=uw if nh == 0 else [], inc="dve")
                u2d = P.c("dve")
                setg("pF1", "dve", u2d)
                setg("pF2", "dve", u2d)
                a = emit_ln([ub[:, c.W5 * i:c.W5 * (i + 1)]
                             for i in range(c.NH)],
                            [ob2[:, c.W5 * i:c.W5 * (i + 1)]
                             for i in range(c.NH)],
                            [("dve", u2d),
                             (f"d_st{par}", st_["st_done"][par])], sset=1)
                u_free[par] = a
                tok = ci * TC + gtt * 128
                P.op("sync", lambda e, s=s, tok=tok, ob2=ob2: e.dma_start(
                    out=out_d[s, tok:tok + 128, :], in_=ob2[:, 0:C]),
                    waits=[("dve", a)], inc=f"d_st{par}", incn=16)
                st_["st_done"][par] = P.c(f"d_st{par}")
                yield
        x2_wr_ok[0] = P.c("dve")

    phase = [0]

    def more():
        phase[0] += 1
        return phase[0] <= TRUNC

    def drain(g, n=10 ** 9):
        k = 0
        while k < n:
            try:
                next(g)
            except StopIteration:
                return None
            k += 1
        return g

    emit_preamble()

    fills = []

    def fill(k):
        while k > 0 and fills:
            try:
                next(fills[0])
                k -= 1
            except StopIteration:
                fills.pop(0)

    def drain_all(g):
        if g in fills:
            for _ in g:
                pass
            fills.remove(g)

    pre_gen = gen_preambleB()
    fills.append(pre_gen)
    prev_chunk = None

    def ffn_stage(half):
        if prev_chunk is None:
            return
        for _ in gen_ffn1(prev_chunk[0], prev_chunk[1], half):
            pass
        fills.append(gen_ffn2(prev_chunk[0], prev_chunk[1], half))

    for s in range(c.S):
        if phase[0] > TRUNC:
            break
        P.op("vector", lambda e: e.memset(khat[:], 0.0),
             waits=[("dve", st_["negm_ready"])], inc="dve")
        vv = Vt[:].rearrange("p (g h v) -> p g h v", g=c.GT, h=H)
        P.op("vector", lambda e, vv=vv: e.memset(vv[:, :, :, 64:65], 1.0),
             waits=[("pe", P.c("pe")), ("dve", ot_evac[0]),
                    ("dve", praw_free[0])], inc="dve")
        ones_done = P.c("dve")
        for ci in range(c.NCH):
            if phase[0] > TRUNC:
                break
            for half in range(2):
                if more():
                    if s == 0 and ci == 0:
                        emit_xt_load(s, ci, half)
                    emit_latent_v(s, ci, half)
                    fill(4)
            if more():
                emit_negm()
            if ci == 0:
                P.op("tensor", lambda e: e.nop(), waits=[("dve", ones_done)])
            for tt in range(min(2, TT)):
                emit_xres_load(s, ci, tt)
            n_attn = max(1, (c.PAIRS // 2) * (TT * (ci + 1) + 2))
            n_ffn = (FC // 2) + 1 + c.HT + 3
            n_pre = FC + CC * (c.F // C)
            est = (n_ffn if any(g is not pre_gen for g in fills) else 0) + \
                  (n_pre // 2 if pre_gen in fills else 0)
            ratio = est / n_attn
            acc = [0.0]
            pf_done = [False]
            for p in range(c.PAIRS):
                if not more():
                    break
                for _ in gen_attn_pair(s, ci, p):
                    acc[0] += ratio
                    k = int(acc[0])
                    if k > 0:
                        acc[0] -= k
                        fill(k)
                if p == c.PAIRS // 2 - 1 and not pf_done[0]:
                    pf_done[0] = True
                    if ci + 1 < c.NCH:
                        emit_xt_load(s, ci + 1, 0)
                        emit_xt_load(s, ci + 1, 1)
                    elif s + 1 < c.S:
                        emit_xt_load(s + 1, 0, 0)
                        emit_xt_load(s + 1, 0, 1)
                    for g in list(fills):
                        if g is not pre_gen:
                            drain_all(g)
                    ffn_stage(1)
                    if prev_chunk is not None and not q1g:
                        issue1g()
            for g in list(fills):
                drain_all(g)
            if more():
                emit_oproj_ln1(s, ci, range(0, c.HT))
                prev_chunk = (s, ci)
                ffn_stage(0)
                emit_oproj_ln1(s, ci, range(c.HT, TT))
    for g in list(fills):
        drain_all(g)
    ffn_stage(1)
    for g in list(fills):
        drain_all(g)

    P.op("sync", lambda e: e.nop(),
         waits=[("d_st0", st_["st_done"][0]), ("d_st1", st_["st_done"][1])])

    with es:
        with nc.Block() as block:
            @block.sync
            def _(e):
                P.run("sync", e, sems)

            @block.gpsimd
            def _(e):
                P.run("gpsimd", e, sems)

            @block.tensor
            def _(e):
                P.run("tensor", e, sems)

            @block.vector
            def _(e):
                P.run("vector", e, sems)

            @block.scalar
            def _(e):
                P.run("scalar", e, sems)

    return nc


def _prep_core_inputs(cfg, hs, bw, core, hr, vw, vb, ow, ob, w1, b1, w2, b2):
    c = cfg
    cn = np.ascontiguousarray
    f = np.float32
    return {
        "hsT": cn(hs.transpose(0, 2, 1)).astype(f),
        "hs": cn(hs).astype(f),
        "bwT": cn(bw.T).astype(f),
        "coreA": cn(core).astype(f),
        "coreB": cn(core.T).astype(f),
        "hrr": cn(hr.transpose(1, 0, 2).reshape(c.R, c.H * c.R)).astype(f),
        "vwT": cn(vw.T).astype(f),
        "owT": cn(ow.T).astype(f),
        "vbr": cn(np.asarray(vb).reshape(1, c.C)).astype(f),
        "obr": cn(np.asarray(ob).reshape(1, c.C)).astype(f),
        "w1n": cn(w1).astype(f),
        "w1t": cn(w1.T).astype(f),
        "b1c": cn(np.asarray(b1).reshape(c.FC, 128).T).astype(f),
        "w2n": cn(w2).astype(f),
        "w2t": cn(w2.T).astype(f),
        "b2c": cn(np.asarray(b2).reshape(c.CC, 128).T).astype(f),
    }


def kernel(L0_attn_hidden_states, L0_attn_basis_w, L0_attn_core,
           L0_attn_head_residual, L0_attn_v_w, L0_attn_v_b, L0_attn_o_w,
           L0_attn_o_b, L0_ffn_w1, L0_ffn_b1, L0_ffn_w2, L0_ffn_b2):
    cfg = CFG()
    n = 8
    hs = np.asarray(L0_attn_hidden_states, dtype=np.float32)
    args = [np.asarray(a, dtype=np.float32) for a in
            (L0_attn_basis_w, L0_attn_core, L0_attn_head_residual,
             L0_attn_v_w, L0_attn_v_b, L0_attn_o_w, L0_attn_o_b,
             L0_ffn_w1, L0_ffn_b1, L0_ffn_w2, L0_ffn_b2)]
    nc = build_program(cfg)
    in_maps = [_prep_core_inputs(cfg, hs[cfg.S * i:cfg.S * (i + 1)], *args)
               for i in range(n)]
    res = run_bass_kernel_spmd(nc, in_maps, core_ids=list(range(n)))
    return np.concatenate([r["out"] for r in res.results], axis=0)


# version 22
# speedup vs baseline: 1.0463x; 1.0463x over previous
import contextlib
import numpy as np
import concourse.bass as bass
import concourse.mybir as mybir
from concourse.bass_utils import run_bass_kernel_spmd
from concourse import library_config

F32 = mybir.dt.float32
BF16 = mybir.dt.bfloat16
FP8 = mybir.dt.float8e4
AF = mybir.ActivationFunctionType
ALU = mybir.AluOpType
AX = mybir.AxisListType

TRUNC = 10 ** 9
SUBT = 10 ** 9


class CFG:
    def __init__(self, S=2, T=2048, C=1024, F=4096, TC=512):
        self.S, self.T, self.C, self.F, self.TC = S, T, C, F, TC
        self.R = 64
        self.H = C // 64
        self.CC = C // 128
        self.FC = F // 128
        self.NCH = T // TC
        self.TT = TC // 128
        self.PAIRS = self.H // 2
        self.HF = TC // 2
        self.HT = self.HF // 128
        self.NH = max(1, C // 512)
        self.W5 = min(C, 512)
        self.GT = T // 128


class Prog:

    def __init__(self):
        self.ops = []
        self.cnt = {}

    def c(self, sem):
        return self.cnt.get(sem, 0)

    def op(self, eng, fn, waits=(), inc=None, incn=1):
        w = [(s, v) for (s, v) in waits if v > 0]
        if inc is not None:
            self.cnt[inc] = self.cnt.get(inc, 0) + incn
        self.ops.append((eng, fn, w, (inc, incn) if inc else None))
        return self.cnt.get(inc, 0) if inc else None

    def run(self, engname, e, sems):
        lastw = {}
        for (en, fn, waits, inc) in self.ops:
            if en != engname:
                continue
            for (s, v) in waits:
                if lastw.get(s, 0) < v:
                    e.wait_ge(sems[s], v)
                    lastw[s] = v
            inst = fn(e)
            if inc is not None:
                inst.then_inc(sems[inc[0]], inc[1])


SEMS = ["pe", "act", "dve", "pool",
        "d_pre", "d_lat0", "d_lat1", "d_hm",
        "d_xr0", "d_xr1", "d_st0", "d_st1",
        "d_wn0", "d_wn1", "d_wn2", "d_wn3",
        "d_wt0", "d_wt1", "d_wt2", "d_wt3",
        "g_xt0", "g_xt1", "g_sw10", "g_sw11", "g_sw20", "g_sw21",
        "g_wl", "g_swW0", "g_swW1"]


def build_program(cfg: CFG):
    nc = bass.Bass("TRN2", target_bir_lowering=False, debug=False)
    c = cfg
    H, R, C, T, TC, CC, FC, TT = c.H, c.R, c.C, c.T, c.TC, c.CC, c.FC, c.TT
    HW = TC // 2

    def din(name, shape, dt=F32):
        return nc.dram_tensor(name, shape, dt, kind="ExternalInput").ap()

    hsT = din("hsT", [c.S, C, T])
    hs = din("hs", [c.S, T, C])
    bwT = din("bwT", [C, R])
    coreA = din("coreA", [R, R])
    coreB = din("coreB", [R, R])
    hrr = din("hrr", [R, H * R])
    vwT = din("vwT", [C, C])
    owT = din("owT", [C, C])
    vbr = din("vbr", [1, C])
    obr = din("obr", [1, C])
    w1n = din("w1n", [c.F, C])
    w1t = din("w1t", [C, c.F])
    b1c_d = din("b1c", [128, FC])
    w2n = din("w2n", [C, c.F])
    w2t = din("w2t", [c.F, C])
    b2c_d = din("b2c", [128, CC])
    out_d = nc.dram_tensor("out", [c.S, T, C], F32, kind="ExternalOutput").ap()
    sw1scr = nc.dram_tensor("sw1scr", [C, c.F], FP8, kind="Internal").ap()
    sw2scr = nc.dram_tensor("sw2scr", [c.F, C], FP8, kind="Internal").ap()

    es = contextlib.ExitStack()

    def sb(name, shape, dt):
        return es.enter_context(nc.sbuf_tensor(name, shape, dt))

    def psum(name, shape, dt=F32):
        return es.enter_context(nc.psum_tensor(name, shape, dt))

    vwTs = sb("vwTs", [128, CC, C], BF16)
    owTs = sb("owTs", [128, CC, C], BF16)
    bwTs = sb("bwTs", [128, CC, R], BF16)
    hm2 = sb("hm2", [128, H * 64], BF16)
    vb_b = sb("vb_b", [128, C], BF16)
    ob_b = sb("ob_b", [128, C], F32)
    al1c = sb("al1c", [128, FC], F32)
    b1c = sb("b1c_s", [128, FC], F32)
    al2c = sb("al2c", [128, CC], F32)
    b2c = sb("b2c_s", [128, CC], F32)
    masks = sb("masks", [128, TT, TC], BF16)
    id16 = sb("id16", [128, 128], BF16)
    id32 = sb("id32", [128, 128], F32)
    ones64 = sb("ones64", [64, 1], BF16)
    consts = sb("consts", [128, 8], F32)

    f2r = sb("f2r", [1, H], F32)
    sqfr = sb("sqfr", [1, H], F32)
    halfr = sb("halfr", [1, H], F32)
    smax = sb("smax", [1, 1], F32)
    khat = sb("khat", [1, 1], F32)
    negm1 = sb("negm1", [1, 1], F32)
    negmc = sb("negmc", [128, 1], F32)


    latd = sb("latd", [128, T], BF16)
    Vt = sb("Vt", [128, c.GT * H * 65], BF16)
    n2row = sb("n2row", [1, TC], F32)
    xT = [sb(f"xT{i}", [128, CC, HW], BF16) for i in range(2)]
    xres = [sb(f"xres{i}", [128, C], F32) for i in range(2)]
    qp = sb("qp", [128, TC], BF16)
    Pp = [sb(f"Pp{i}", [128, 2 * TC], BF16) for i in range(2)]
    Praw = sb("Praw", [128, 2 * TC], BF16)
    lbx = sb("lbx", [128, H * R], F32)
    lrec = sb("lrec", [128, 2 * TC], F32)
    lbf = lrec[0:64, :]
    l16 = lrec[64:128, :].bitcast(BF16)[:, 0:2 * TC]
    onesb = sb("onesb", [128, 128], BF16)
    negm16 = sb("negm16", [1, 1], BF16)
    OTf = sb("OTf", [128, c.PAIRS, TC], BF16)
    ubuf = [sb(f"ubuf{i}", [128, C], F32) for i in range(2)]
    iosc = ubuf[0][:, 0:max(TC, 128)]

    outb = [sb(f"outb{i}", [128, C], F32) for i in range(2)]
    if C >= 768:
        presc = outb[0][0:64, 0:768]
    else:
        presc = sb("presc", [64, 768], F32)[0:64, :]

    mvb = sb("mvb", [128, 4], F32)
    sbb = sb("sbb", [128, 8], F32)
    bnsc = sb("bnsc", [128, 12 * ((C + 511) // 512)], F32)
    x2 = sb("x2", [128, TT, C], F32)
    x2b = sb("x2b", [128, C], BF16)
    sq64 = x2b[0:64, 0:C]

    x2T = sb("x2T", [128, CC, TC], BF16)
    hbuf = sb("hbuf", [128, FC, c.HF], BF16)
    sw1b = [sb(f"sw1b{i}", [128, CC, 512], FP8) for i in range(2)]
    sw2b = [sb(f"sw2b{i}", [128, 2, C], FP8) for i in range(2)]
    y2t = sb("y2t", [128, CC, c.HF], F32)[:]
    if FC * c.HF // 2 >= 4 * C and CC * c.HF >= 2 * C:
        _hb32 = hbuf[:].rearrange("p a b -> p (a b)").bitcast(F32)
        _y32 = y2t[:].rearrange("p a b -> p (a b)")
        wna = [_hb32[:, 0:C], _hb32[:, C:2 * C],
               outb[1][:, 0:C], ubuf[1][:, 0:C]]
        wnt = [_hb32[:, 2 * C:3 * C], _hb32[:, 3 * C:4 * C],
               _y32[:, 0:C], _y32[:, C:2 * C]]
    else:
        wna = [sb(f"wna{i}", [128, C], F32)[:] for i in range(4)]
        wnt = [sb(f"wnt{i}", [128, C], F32)[:] for i in range(4)]
    _x2t16 = x2T[:].rearrange("p a b -> p (a b)")
    wsg = [_x2t16[:, C * i:C * (i + 1)] for i in range(2)]

    pSW = psum("pSW", [128, 1024], F32)
    pS = pSW[:, 0:512]
    pS2 = pSW[:, 512:1024]
    pSX = psum("pSX", [128, 1024], F32)
    pS3 = pSX[:, 0:512]
    pS4 = pSX[:, 512:1024]
    pO = psum("pO", [128, 1024], F32)
    pFW = psum("pFW", [128, 1024], F32)
    pF1 = pFW[:, 0:512]
    pF2 = pFW[:, 512:1024]
    SBK = [(pS, pS2), (pS3, pS4)]
    SBKW = [pSW, pSX]
    _b2 = [pF1, pF2]
    _b2n = ["pF1", "pF2"]
    NP2 = max(1, (CC * c.HF) // 1024)
    DCP = CC // NP2
    y2banks = [(_b2[((dc % DCP) * c.HF) // 512],
                ((dc % DCP) * c.HF) % 512,
                _b2n[((dc % DCP) * c.HF) // 512]) for dc in range(CC)]
    assert DCP * c.HF <= 2 * 512

    sems = {s: es.enter_context(nc.semaphore(s)) for s in SEMS}
    P = Prog()

    bankg = {k: [] for k in
             ["pS", "pS2", "pS3", "pS4", "pOa", "pOb", "pF1", "pF2"]}

    def guard(*names):
        w = []
        for n in names:
            w += bankg[n]
        return w

    def setg(name, sem, val):
        bankg[name] = [(sem, val)]

    lat_evac = [0]
    n2_evac = [0]
    q_evac = [0]
    ot_evac = [0]
    latdup_cnt = [0, 0]
    latdup_order = [0]
    x2t_evac = [0]
    x2b_free = [0]
    u_free = [0, 0]
    u2_free = [0]
    x2_wr_ok = [0]
    otn_done = [0]
    s_done = [("pe", 0), ("pe", 0)]
    x2T_rd_free = [0]
    y2t_free = [0]
    outb_free = [0, 0]
    h_free = [0, 0]
    praw_free = [0]
    st_ = {
        "xt_load": [0, 0], "xt_free": [0, 0],
        "xr_load": [0, 0], "xr_free": [0, 0],
        "st_done": [0, 0],
        "sw1_load": [0, 0], "sw1_free": [0, 0],
        "sw2_load": [0, 0], "sw2_free": [0, 0],
        "pp_free": [0, 0],
        "negm_ready": 0, "x2_ready": 0, "x2T_ready": 0, "h_done": 0,
        "x2T_half": [0, 0],
    }
    pre = {'sw1_blk': {}, 'sw2_blk': {}, 'al1_ready': {}}
    wn_free = [0, 0, 0, 0]
    wt_free = [0, 0, 0, 0]
    sg_free = [0, 0]

    def emit_preamble():
        P.op("gpsimd", lambda e: e.iota(
            iosc[:, 0:128], pattern=[[1, 128]], base=0, channel_multiplier=-1,
            allow_small_or_imprecise_dtypes=True), inc="pool")
        P.op("vector", lambda e: e.tensor_scalar(
            id16[:], iosc[:, 0:128], 0.0, None, op0=ALU.is_equal),
            waits=[("pool", P.c("pool"))], inc="dve")
        P.op("vector", lambda e: e.tensor_scalar(
            id32[:], iosc[:, 0:128], 0.0, None, op0=ALU.is_equal), inc="dve")
        for o in range(TT):
            pv = P.op("gpsimd", lambda e, o=o: e.iota(
                iosc[:, 0:TC], pattern=[[1, TC]], base=-128 * o,
                channel_multiplier=-1, allow_small_or_imprecise_dtypes=True),
                waits=[("dve", P.c("dve"))], inc="pool")
            P.op("vector", lambda e, o=o: e.tensor_scalar(
                masks[:, o, :], iosc[:, 0:TC], 0.0, -60.0, op0=ALU.is_lt,
                op1=ALU.mult), waits=[("pool", pv)], inc="dve")
        P.op("vector", lambda e: e.memset(ones64[:], 1.0), inc="dve")
        P.op("vector", lambda e: e.memset(onesb[:], 1.0), inc="dve")
        P.op("vector", lambda e: e.memset(consts[:, 0:1], -0.5), inc="dve")
        P.op("vector", lambda e: e.memset(halfr[:], 0.5), inc="dve")
        P.op("vector", lambda e: e.memset(khat[:], 0.0), inc="dve")

        for dst, src in [(presc[0:64, 0:R], coreA),
                         (presc[0:64, 64:128], coreB),
                         (ob_b[0:1, :], obr),
                         (b1c[:], b1c_d), (b2c[:], b2c_d),
                         (lbx[0:64, 0:H * R], hrr)]:
            P.op("sync", lambda e, d=dst, s=src: e.dma_start(out=d, in_=s),
                 inc="d_pre", incn=16)
        pre_small = P.c("d_pre")

        for dst, src in [
                (vb_b[0:1, :], vbr),
                (bwTs[:], bwT.rearrange("(cc p) r -> p cc r", p=128)),
                (vwTs[:], vwT.rearrange("(cc p) co -> p cc co", p=128)),
                (owTs[:], owT.rearrange("(cc p) co -> p cc co", p=128))]:
            P.op("gpsimd", lambda e, d=dst, s=src: e.dma_start(out=d, in_=s),
                 inc="g_wl", incn=16)
        pre["wl_done"] = P.c("g_wl")

        for buf in (vb_b, ob_b):
            k = 1
            while k < 128:
                P.op("sync", lambda e, buf=buf, k=k: e.dma_start(
                    out=buf[k:2 * k, :], in_=buf[0:k, :]),
                    waits=[("d_pre", P.c("d_pre")),
                           ("g_wl", pre["wl_done"])], inc="d_pre", incn=16)
                k *= 2
        pre["bcast_done"] = P.c("d_pre")

        P.op("vector", lambda e: e.tensor_add(
            presc[0:64, 128:128 + H * R // 2], lbx[0:64, 0:H * R // 2],
            lbx[0:64, H * R // 2:H * R]),
            waits=[("d_pre", pre_small)], inc="dve")
        width = H * R // 2
        while width > 64:
            w2 = width // 2
            P.op("vector", lambda e, w2=w2, width=width: e.tensor_add(
                presc[0:64, 128:128 + w2], presc[0:64, 128:128 + w2],
                presc[0:64, 128 + w2:128 + width]),
                waits=[("dve", P.c("dve"))], inc="dve")
            width = w2
        P.op("vector", lambda e: e.tensor_add(
            presc[0:64, 704:768], presc[0:64, 0:64], presc[0:64, 64:128]),
            waits=[("dve", P.c("dve"))], inc="dve")
        P.op("vector", lambda e: e.tensor_scalar(
            presc[0:64, 704:768], presc[0:64, 704:768], 0.5 / H / 8.0, None,
            op0=ALU.mult), waits=[("dve", P.c("dve"))], inc="dve")
        P.op("vector", lambda e: e.tensor_scalar(
            presc[0:64, 640:704], presc[0:64, 128:192], -1.0 / H / 8.0, None,
            op0=ALU.mult), waits=[("dve", P.c("dve"))], inc="dve")
        P.op("vector", lambda e: e.tensor_add(
            presc[0:64, 640:704], presc[0:64, 640:704], presc[0:64, 704:768]),
            waits=[("dve", P.c("dve"))], inc="dve")
        for h in range(H):
            P.op("vector", lambda e, h=h: e.scalar_tensor_tensor(
                hm2[0:64, 64 * h:64 * h + 64], lbx[0:64, 64 * h:64 * h + 64],
                0.125, presc[0:64, 640:704], op0=ALU.mult, op1=ALU.add),
                waits=[("dve", P.c("dve"))], inc="dve")
        hm_low = P.c("dve")
        P.op("sync", lambda e: e.dma_start(out=hm2[64:128, :], in_=hm2[0:64, :]),
             waits=[("dve", hm_low)], inc="d_hm", incn=16)
        pre["hm_dup"] = P.c("d_hm")

        P.op("scalar", lambda e: e.activation(
            sq64[0:64, 0:H * 64], hm2[0:64, :], AF.Square),
            waits=[("dve", hm_low)], inc="act")
        sqd = P.c("act")
        nhalf = (H * 64 + 511) // 512
        for i in range(nhalf):
            wdt = min(512, H * 64 - 512 * i)
            P.op("tensor", lambda e, i=i, wdt=wdt: e.matmul(
                pO[0:1, 512 * i:512 * i + wdt], ones64[:],
                sq64[0:64, 512 * i:512 * i + wdt], start=True, stop=True),
                waits=[("act", sqd), ("dve", P.c("dve"))],
                inc="pe" if i == nhalf - 1 else None)
        P.op("vector", lambda e: e.tensor_reduce(
            f2r[:].unsqueeze(2), pO[0:1, 0:H * 64].rearrange(
                "p (h r) -> p h r", h=H), axis=AX.X, op=ALU.add),
            waits=[("pe", P.c("pe"))], inc="dve")
        P.op("gpsimd", lambda e: e.tensor_tensor(
            sqfr[:], f2r[:], halfr[:], op=ALU.pow),
            waits=[("dve", P.c("dve"))], inc="pool")
        P.op("vector", lambda e: e.tensor_reduce(
            smax[:], sqfr[:], axis=AX.X, op=ALU.max),
            waits=[("pool", P.c("pool"))], inc="dve")
        setg("pOa", "dve", P.c("dve"))

        return None

    def gen_preambleB():
        def sign_stage(src_slice, dst_slice, idx):
            par = idx % 4
            sem = f"d_wt{par}"
            wsem = f"g_swW{idx % 2}"
            fw = [("act", wt_free[par])] if wt_free[par] else []
            P.op("scalar", lambda e, s=src_slice, par=par: e.dma_start(
                out=wnt[par], in_=s), waits=fw, inc=sem, incn=16)
            P.op("scalar", lambda e, par=par, sp=idx % 2: e.sign(
                wsg[sp], wnt[par]),
                 waits=[(sem, P.c(sem)), (wsem, sg_free[idx % 2])], inc="act")
            wt_free[par] = P.c("act")
            P.op("gpsimd", lambda e, d=dst_slice, sp=idx % 2: e.dma_start(
                out=d, in_=wsg[sp]), waits=[("act", P.c("act"))],
                inc=wsem, incn=16)
            sg_free[idx % 2] = P.c(wsem)

        idx = 0
        sg1 = [(a, b) for b in range(c.F // C) for a in range(CC)]
        wn_counts = {}
        wt_counts = {}

        def load_wna1(i):
            par = i % 4
            sem = f"d_wn{par}"
            fw = [("dve", wn_free[par])] if wn_free[par] else []
            P.op("sync", lambda e, i=i, par=par: e.dma_start(
                out=wna[par], in_=w1n[128 * i:128 * (i + 1), :]),
                waits=fw, inc=sem, incn=16)
            wn_counts[i] = P.c(sem)

        def load_wnt1(t):
            a, b = sg1[t]
            par = t % 4
            sem = f"d_wt{par}"
            fw = [("act", wt_free[par])] if wt_free[par] else []
            P.op("sync", lambda e, a=a, b=b, par=par: e.dma_start(
                out=wnt[par], in_=w1t[128 * a:128 * (a + 1),
                                      C * b:C * (b + 1)]),
                waits=fw, inc=sem, incn=16)
            wt_counts[t] = P.c(sem)

        LOOK = 3
        for i in range(min(LOOK, FC)):
            load_wna1(i)
            load_wnt1(i)
        for i in range(FC):
            if i + LOOK < FC:
                load_wna1(i + LOOK)
                load_wnt1(i + LOOK)
            par = i % 4
            P.op("vector", lambda e, i=i, par=par: e.tensor_reduce(
                al1c[:, i:i + 1], wna[par], axis=AX.X, op=ALU.add,
                apply_absolute_value=True),
                waits=[(f"d_wn{par}", wn_counts[i])], inc="dve")
            P.op("vector", lambda e, i=i: e.tensor_scalar(
                al1c[:, i:i + 1], al1c[:, i:i + 1], 1.0 / C, None,
                op0=ALU.mult), waits=[("dve", P.c("dve"))], inc="dve")
            wn_free[par] = P.c("dve")
            pre["al1_ready"][i] = P.c("dve")
            if i < len(sg1):
                a, b = sg1[i]
                wsem = f"g_swW{i % 2}"
                P.op("scalar", lambda e, par=par, sp=i % 2: e.sign(
                    wsg[sp], wnt[par]),
                    waits=[(f"d_wt{par}", wt_counts[i]),
                           (wsem, sg_free[i % 2])], inc="act")
                wt_free[par] = P.c("act")
                P.op("gpsimd", lambda e, a=a, b=b, sp=i % 2: e.dma_start(
                    out=sw1scr[128 * a:128 * (a + 1), C * b:C * (b + 1)],
                    in_=wsg[sp]), waits=[("act", P.c("act"))],
                    inc=wsem, incn=16)
                sg_free[i % 2] = P.c(wsem)
                if (i + 1) % CC == 0:
                    pre["sw1_blk"][(i + 1) // CC - 1] = (
                        P.c("g_swW0"), P.c("g_swW1"))
                idx += 1
            yield
        pre["sw1_ready"] = (P.c("g_swW0"), P.c("g_swW1"))
        for b in range(c.F // C):
            if b not in pre["sw1_blk"]:
                pre["sw1_blk"][b] = pre["sw1_ready"]

        wn2c = {}
        wt2c = {}

        def load_wna2(j):
            i, b = divmod(j, c.F // C)
            par = j % 4
            sem = f"d_wn{par}"
            P.op("sync", lambda e, i=i, b=b, par=par: e.dma_start(
                out=wna[par], in_=w2n[128 * i:128 * (i + 1),
                                      C * b:C * (b + 1)]),
                waits=[("dve", wn_free[par])], inc=sem, incn=16)
            wn2c[j] = P.c(sem)

        def load_wnt2(t):
            par = t % 4
            sem = f"d_wt{par}"
            fw = [("act", wt_free[par])] if wt_free[par] else []
            P.op("sync", lambda e, t=t, par=par: e.dma_start(
                out=wnt[par], in_=w2t[128 * t:128 * (t + 1), :]),
                waits=fw, inc=sem, incn=16)
            wt2c[t] = P.c(sem)

        NW2 = CC * (c.F // C)
        for j in range(min(3, NW2)):
            load_wna2(j)
            load_wnt2(j)
        for j in range(NW2):
            if j + 3 < NW2:
                load_wna2(j + 3)
                load_wnt2(j + 3)
            i, b = divmod(j, c.F // C)
            par = j % 4
            if b == 0:
                P.op("vector", lambda e, i=i, par=par: e.tensor_reduce(
                    al2c[:, i:i + 1], wna[par], axis=AX.X, op=ALU.add,
                    apply_absolute_value=True),
                    waits=[(f"d_wn{par}", wn2c[j])], inc="dve")
            else:
                P.op("vector", lambda e, par=par: e.tensor_reduce(
                    consts[:, 1:2], wna[par], axis=AX.X, op=ALU.add,
                    apply_absolute_value=True),
                    waits=[(f"d_wn{par}", wn2c[j]), ("dve", P.c("dve"))],
                    inc="dve")
                P.op("vector", lambda e, i=i: e.tensor_add(
                    al2c[:, i:i + 1], al2c[:, i:i + 1], consts[:, 1:2]),
                    waits=[("dve", P.c("dve"))], inc="dve")
            wn_free[par] = P.c("dve")
            if j < FC:
                wsem = f"g_swW{j % 2}"
                P.op("scalar", lambda e, par=par, sp=j % 2: e.sign(
                    wsg[sp], wnt[par]),
                    waits=[(f"d_wt{par}", wt2c[j]),
                           (wsem, sg_free[j % 2])], inc="act")
                wt_free[par] = P.c("act")
                P.op("gpsimd", lambda e, t=j, sp=j % 2: e.dma_start(
                    out=sw2scr[128 * t:128 * (t + 1), :],
                    in_=wsg[sp]), waits=[("act", P.c("act"))],
                    inc=wsem, incn=16)
                sg_free[j % 2] = P.c(wsem)
                if (j + 1) % 2 == 0:
                    pre["sw2_blk"][(j + 1) // 2 - 1] = (
                        P.c("g_swW0"), P.c("g_swW1"))
                idx += 1
            yield
        P.op("vector", lambda e: e.tensor_scalar(
            al2c[:], al2c[:], 1.0 / c.F, None, op0=ALU.mult),
            waits=[("dve", P.c("dve"))], inc="dve")
        pre["sw2_ready"] = (P.c("g_swW0"), P.c("g_swW1"))
        pre["al2_ready"] = P.c("dve")
        pre["wn_done2"] = P.c("d_wn2")
        pre["wn_done3"] = P.c("d_wn3")
        for b in range(FC // 2):
            if b not in pre["sw2_blk"]:
                pre["sw2_blk"][b] = pre["sw2_ready"]

    def emit_xt_load(s, ci, half):
        par = (ci * 2 + half) % 2
        off = ci * TC + half * HW
        sem = f"g_xt{par}"
        waits = [("pe", st_["xt_free"][par])] if st_["xt_load"][par] else []
        src = hsT[s, :, off:off + HW].rearrange("(cc p) t -> p cc t", p=128)
        P.op("gpsimd", lambda e, par=par, src=src: e.dma_start(
            out=xT[par][:], in_=src), waits=waits, inc=sem, incn=16)
        st_["xt_load"][par] = P.c(sem)

    def emit_latent_v(s, ci, half):
        par = (ci * 2 + half) % 2
        off = ci * TC + half * HW
        sem = f"g_xt{par}"
        first_w = [(sem, st_["xt_load"][par]), ("g_wl", pre["wl_done"]),
                   ("dve", lat_evac[0])] + guard("pS")
        for k in range(CC):
            P.op("tensor", lambda e, k=k, par=par: e.matmul(
                pS[0:64, 0:HW], bwTs[:, k, :], xT[par][:, k, :],
                start=(k == 0), stop=(k == CC - 1)),
                waits=first_w if k == 0 else [],
                inc="pe" if k == CC - 1 else None)
        lat_mm = P.c("pe")
        if SUBT < 2:
            return
        P.op("vector", lambda e, off=off: e.tensor_copy(
            latd[0:64, off:off + HW], pS[0:64, 0:HW]),
            waits=[("pe", lat_mm)], inc="dve")
        lat_evac[0] = P.c("dve")
        setg("pS", "dve", lat_evac[0])
        if SUBT < 3:
            return
        P.op("scalar", lambda e: e.activation(
            sq64[0:64, 0:HW], pS[0:64, 0:HW], AF.Square),
            waits=[("pe", lat_mm), ("dve", lat_evac[0])], inc="act")
        setg("pS", "act", P.c("act"))
        P.op("tensor", lambda e: e.matmul(
            pS2[0:1, 0:HW], ones64[:], sq64[0:64, 0:HW], start=True,
            stop=True), waits=[("act", P.c("act")), ("dve", n2_evac[0])]
            + guard("pS2"), inc="pe")
        P.op("vector", lambda e, half=half: e.tensor_copy(
            n2row[0:1, half * HW:(half + 1) * HW], pS2[0:1, 0:HW]),
            waits=[("pe", P.c("pe"))], inc="dve")
        n2_evac[0] = P.c("dve")
        setg("pS2", "dve", n2_evac[0])
        if SUBT < 4:
            return
        dsem = f"d_lat{par}"
        P.op("sync", lambda e, off=off: e.dma_start(
            out=latd[64:128, off:off + HW], in_=latd[0:64, off:off + HW]),
            waits=[("dve", lat_evac[0]), ("pe", latdup_order[0])],
            inc=dsem, incn=16)
        latdup_cnt[par] = P.c(dsem)
        if SUBT < 5:
            return
        vv = Vt[:].rearrange("p (g h v) -> p g h v", g=c.GT, h=H)
        for tt in range(HW // 128):
            gt = off // 128 + tt
            for coh in range(c.NH):
                pob = "pOa" if coh % 2 == 0 else "pOb"
                for k in range(CC):
                    P.op("tensor", lambda e, tt=tt, coh=coh, k=k, par=par: e.matmul(
                        pO[0:128, 512 * (coh % 2):512 * (coh % 2) + c.W5],
                        xT[par][:, k, 128 * tt:128 * tt + 128],
                        vwTs[:, k, c.W5 * coh:c.W5 * (coh + 1)],
                        start=(k == 0), stop=(k == CC - 1)),
                        waits=(guard(pob) + [("d_pre", pre["bcast_done"])]
                               if k == 0 else []),
                        inc="pe" if k == CC - 1 else None)
                vm = P.c("pe")
                hpg = c.W5 // 64
                P.op("vector", lambda e, gt=gt, coh=coh, vv=vv, hpg=hpg: e.tensor_add(
                    vv[:, gt, coh * hpg:(coh + 1) * hpg, 0:64],
                    pO[0:128, 512 * (coh % 2):512 * (coh % 2) + c.W5]
                    .rearrange("p (h v) -> p h v", v=64),
                    vb_b[:, c.W5 * coh:c.W5 * (coh + 1)]
                    .rearrange("p (h v) -> p h v", v=64)),
                    waits=[("pe", vm)], inc="dve")
                setg(pob, "dve", P.c("dve"))
        st_["xt_free"][par] = P.c("pe")

    def emit_negm():
        P.op("vector", lambda e: e.tensor_reduce(
            consts[0:1, 2:3], n2row[:], axis=AX.X, op=ALU.max),
            waits=[("dve", n2_evac[0]), ("dve", st_["negm_ready"])],
            inc="dve")
        P.op("vector", lambda e: e.tensor_tensor(
            khat[:], khat[:], consts[0:1, 2:3], op=ALU.max),
            waits=[("dve", P.c("dve"))], inc="dve")
        P.op("vector", lambda e: e.tensor_scalar(
            negm1[:], khat[:], smax[0:1, 0:1], -1.02, op0=ALU.mult,
            op1=ALU.mult), waits=[("dve", P.c("dve"))], inc="dve")
        P.op("vector", lambda e: e.tensor_copy(negm16[:], negm1[:]),
             waits=[("dve", P.c("dve"))], inc="dve")
        P.op("tensor", lambda e: e.matmul(
            pO[0:128, 0:1], onesb[0:1, 0:128], negm16[:], start=True,
            stop=True), waits=[("dve", P.c("dve"))] + guard("pOa"), inc="pe")
        P.op("vector", lambda e: e.tensor_copy(negmc[:], pO[0:128, 0:1]),
             waits=[("pe", P.c("pe"))], inc="dve")
        setg("pOa", "dve", P.c("dve"))
        st_["negm_ready"] = P.c("dve")

    def gen_attn_pair(s, ci, p):
        JT = TT * (ci + 1)
        coff = ci * TC
        h0, h1 = 2 * p, 2 * p + 1
        ldw = [("d_lat0", latdup_cnt[0]), ("d_lat1", latdup_cnt[1]),
               ("d_hm", pre["hm_dup"])]
        P.op("tensor", lambda e, h0=h0, coff=coff: e.matmul(
            pO[0:64, 0:TC], hm2[0:64, 64 * h0:64 * h0 + 64],
            latd[0:64, coff:coff + TC], start=True, stop=True,
            tile_position=(0, 0)),
            waits=ldw + [("act", q_evac[0])] + guard("pOa", "pOb"), inc="pe")
        P.op("tensor", lambda e, h1=h1, coff=coff: e.matmul(
            pO[64:128, 0:TC], hm2[64:128, 64 * h1:64 * h1 + 64],
            latd[64:128, coff:coff + TC], start=True, stop=True,
            tile_position=(64, 64)), inc="pe")
        qm = P.c("pe")
        latdup_order[0] = qm
        P.op("scalar", lambda e: e.activation(qp[:], pO[0:128, 0:TC], AF.Copy),
             waits=[("pe", qm)], inc="act")
        q_evac[0] = P.c("act")
        setg("pOa", "act", q_evac[0])

        def joff(j):
            return 128 * (j - TT * ci) if j >= TT * ci else 0

        def emit_S(j):
            par = j % 2
            bA, bB = SBK[par]
            gA = "pS" if par == 0 else "pS3"
            gB = "pS2" if par == 0 else "pS4"
            off = joff(j)
            P.op("tensor", lambda e, j=j, bA=bA, off=off: e.matmul(
                bA[0:128, off:TC], latd[0:64, 128 * j:128 * j + 128],
                qp[0:64, off:TC], start=True, stop=True,
                tile_position=(0, 0)),
                waits=[("act", q_evac[0])] + guard(gA, gB), inc="pe")
            P.op("tensor", lambda e, j=j, bB=bB, off=off: e.matmul(
                bB[0:128, off:TC], latd[64:128, 128 * j:128 * j + 128],
                qp[64:128, off:TC], start=True, stop=True,
                tile_position=(64, 0)),
                inc="pe")
            sm = P.c("pe")
            if j >= TT * ci:
                o = j - TT * ci
                P.op("vector", lambda e, o=o, bA=bA, off=off:
                     e.scalar_tensor_tensor(
                         bA[:, off:TC], bA[:, off:TC], 1.0,
                         masks[:, o, off:TC],
                         op0=ALU.mult, op1=ALU.add), waits=[("pe", sm)],
                     inc="dve")
                P.op("vector", lambda e, o=o, bB=bB, off=off:
                     e.scalar_tensor_tensor(
                         bB[:, off:TC], bB[:, off:TC], 1.0,
                         masks[:, o, off:TC],
                         op0=ALU.mult, op1=ALU.add), inc="dve")
                s_done[j % 2] = ("dve", P.c("dve"))
            else:
                s_done[j % 2] = ("pe", sm)

        emit_S(0)
        if JT > 1:
            emit_S(1)
        for j in range(JT):
            par = j % 2
            bA, bB = SBK[par]
            gA = "pS" if par == 0 else "pS3"
            gB = "pS2" if par == 0 else "pS4"
            dst = Pp[par]
            off = joff(j)
            ew = [s_done[par], ("dve", st_["negm_ready"]),
                  ("pe", st_["pp_free"][par])]
            if TC == 512 and off == 0:
                P.op("scalar", lambda e, dst=dst, par=par: e.activation(
                    dst[:, 0:2 * TC], SBKW[par][:, 0:1024], AF.Exp,
                    bias=negmc[:, 0:1]), waits=ew, inc="act")
            else:
                P.op("scalar", lambda e, bA=bA, dst=dst, off=off: e.activation(
                    dst[:, off:TC], bA[:, off:TC], AF.Exp,
                    bias=negmc[:, 0:1]), waits=ew, inc="act")
                P.op("scalar", lambda e, bB=bB, dst=dst, off=off: e.activation(
                    dst[:, TC + off:2 * TC], bB[:, off:TC], AF.Exp,
                    bias=negmc[:, 0:1]), inc="act")
            ed = P.c("act")
            setg(gA, "act", ed)
            setg(gB, "act", ed)
            if j + 2 < JT:
                emit_S(j + 2)
            vv = Vt[:].rearrange("p (g h v) -> p g h v", g=c.GT, h=H)
            P.op("tensor", lambda e, j=j, par=par, vv=vv, h0=h0, JT=JT,
                 off=off: e.matmul(
                     pO[0:65, off:TC], vv[:, j, h0, :], Pp[par][:, off:TC],
                     start=(j == 0), stop=(j == JT - 1)),
                 waits=[("act", ed)], inc="pe")
            P.op("tensor", lambda e, j=j, par=par, vv=vv, h1=h1, JT=JT,
                 off=off: e.matmul(
                     pO[0:65, 512 + off:512 + TC], vv[:, j, h1, :],
                     Pp[par][:, TC + off:2 * TC],
                     start=(j == 0), stop=(j == JT - 1)), inc="pe")
            st_["pp_free"][par] = P.c("pe")
            yield
        pvd = P.c("pe")
        P.op("vector", lambda e: e.tensor_copy(
            l16[0:1, :].rearrange("p (b t) -> p b t", b=2),
            pO[64:65, :].rearrange("p (b t) -> p b t", b=2)[:, :, 0:TC]),
            waits=[("pe", pvd), ("dve", praw_free[0])], inc="dve")
        lc = P.c("dve")
        P.op("vector", lambda e: e.tensor_copy(
            Praw[0:64, 0:TC], pO[0:64, 0:TC]), inc="dve")
        P.op("vector", lambda e: e.tensor_copy(
            Praw[64:128, 0:TC], pO[0:64, 512:512 + TC]), inc="dve")
        ot_evac[0] = P.c("dve")
        setg("pOa", "dve", ot_evac[0])
        setg("pOb", "dve", ot_evac[0])
        P.op("tensor", lambda e: e.matmul(
            pS[0:64, 0:TC], onesb[64:65, 0:64], l16[0:1, 0:TC],
            start=True, stop=True, tile_position=(64, 0)),
            waits=[("dve", lc)] + guard("pS", "pS2"), inc="pe")
        P.op("tensor", lambda e: e.matmul(
            pSW[64:128, 0:TC], onesb[64:65, 0:64], l16[0:1, TC:2 * TC],
            start=True, stop=True, tile_position=(64, 64)), inc="pe")
        lb = P.c("pe")
        yield
        P.op("vector", lambda e: e.tensor_copy(
            lrec[:, 0:TC], pSW[0:128, 0:TC]), waits=[("pe", lb)], inc="dve")
        lbe = P.c("dve")
        setg("pS", "dve", lbe)
        setg("pS2", "dve", lbe)
        P.op("vector", lambda e: e.reciprocal(
            lrec[:, TC:2 * TC], lrec[:, 0:TC]),
            waits=[("dve", lbe)], inc="dve")
        lbc = P.c("dve")
        P.op("vector", lambda e, p=p: e.tensor_mul(
            OTf[:, p, :], Praw[:, 0:TC], lrec[:, TC:2 * TC]),
            waits=[("dve", lbc)], inc="dve")
        praw_free[0] = P.c("dve")
        otn_done[0] = P.c("dve")
        yield

    def emit_xres_load(s, ci, tt):
        par = tt % 2
        sem = f"d_xr{par}"
        waits = [("dve", st_["xr_free"][par])] if st_["xr_load"][par] else []
        tok = ci * TC + tt * 128
        P.op("sync", lambda e, s=s, tok=tok, par=par: e.dma_start(
            out=xres[par][:], in_=hs[s, tok:tok + 128, :]),
            waits=waits, inc=sem, incn=16)
        st_["xr_load"][par] = P.c(sem)

    def emit_ln(u_parts, dst_parts, w_pre, sset=0):
        nst = len(u_parts)
        bo = 12 * ((C + 511) // 512) // 2 * sset
        mo, so = 2 * sset, 4 * sset
        for i, ua in enumerate(u_parts):
            P.op("vector", lambda e, ua=ua, i=i: e.bn_stats(
                bnsc[:, bo + 6 * i:bo + 6 * (i + 1)], ua),
                waits=(list(w_pre) if i == 0 else []), inc="dve")
        P.op("vector", lambda e, nst=nst: e.bn_aggr(
            mvb[:, mo:mo + 2], bnsc[:, bo:bo + 6 * nst]),
            waits=[("dve", P.c("dve"))], inc="dve")
        P.op("vector", lambda e: e.tensor_scalar(
            sbb[:, so:so + 1], mvb[:, mo + 1:mo + 2], 1e-5, None, op0=ALU.add),
            waits=[("dve", P.c("dve"))], inc="dve")
        P.op("gpsimd", lambda e: e.tensor_tensor(
            sbb[:, so + 1:so + 2], sbb[:, so:so + 1], consts[:, 0:1],
            op=ALU.pow), waits=[("dve", P.c("dve"))], inc="pool")
        pw = P.c("pool")
        for i, (ua, da) in enumerate(zip(u_parts, dst_parts)):
            P.op("vector", lambda e, ua=ua, da=da: e.tensor_scalar(
                da, ua, mvb[:, mo:mo + 1], sbb[:, so + 1:so + 2],
                op0=ALU.subtract, op1=ALU.mult),
                waits=([("pool", pw)] if i == 0 else []), inc="dve")
        return P.c("dve")

    def emit_oproj_ln1(s, ci, tiles=None):
        for tt in (range(TT) if tiles is None else tiles):
            par = tt % 2
            for coh in range(c.NH):
                dstp = pS if coh % 2 == 0 else pS2
                gname = "pS" if coh % 2 == 0 else "pS2"
                for p in range(c.PAIRS):
                    P.op("tensor", lambda e, p=p, tt=tt, coh=coh, dstp=dstp:
                         e.matmul(
                             dstp[:, 0:c.W5],
                             OTf[:, p, 128 * tt:128 * tt + 128],
                             owTs[:, p, c.W5 * coh:c.W5 * (coh + 1)],
                             start=(p == 0), stop=(p == c.PAIRS - 1)),
                         waits=([("dve", otn_done[0])] + guard(gname)
                                if p == 0 else []),
                         inc="pe" if p == c.PAIRS - 1 else None)
            ym = P.c("pe")
            ub = ubuf[par]
            P.op("vector", lambda e, ub=ub, par=par: e.tensor_add(
                ub[:, 0:C], xres[par][:, 0:C], ob_b[:, 0:C]),
                waits=[(f"d_xr{par}", st_["xr_load"][par]),
                       ("dve", u_free[par]),
                       ("d_wn3", pre.get("wn_done3", 0) if par else 0)],
                inc="dve")
            st_["xr_free"][par] = P.c("dve")
            if C <= 1024:
                P.op("vector", lambda e, ub=ub: e.tensor_add(
                    ub[:, 0:C], pSW[:, 0:C], ub[:, 0:C]),
                    waits=[("pe", ym), ("dve", P.c("dve"))], inc="dve")
            setg("pS", "dve", P.c("dve"))
            setg("pS2", "dve", P.c("dve"))
            ud = P.c("dve")
            a = emit_ln([ub[:, c.W5 * i:c.W5 * (i + 1)] for i in range(c.NH)],
                        [x2[:, tt, c.W5 * i:c.W5 * (i + 1)]
                         for i in range(c.NH)],
                        [("dve", ud), ("dve", x2_wr_ok[0])], sset=0)
            u_free[par] = a
            if tt + 2 < TT:
                emit_xres_load(s, ci, tt + 2)
            P.op("vector", lambda e, tt=tt: e.tensor_copy(x2b[:], x2[:, tt, :]),
                 waits=[("dve", a), ("pe", x2b_free[0]),
                        ("pe", x2T_rd_free[0])], inc="dve")
            cb = P.c("dve")
            for k in range(CC):
                tb = pF1 if k % 2 == 0 else pF2
                gname = "pF1" if k % 2 == 0 else "pF2"
                P.op("tensor", lambda e, k=k, tb=tb: e.transpose(
                    tb[:, 0:64].bitcast(BF16), x2b[:, 128 * k:128 * k + 128],
                    id16[:]),
                    waits=[("dve", cb)] + guard(gname), inc="pe")
                P.op("vector", lambda e, k=k, tt=tt, tb=tb: e.tensor_copy(
                    x2T[:, k, 128 * tt:128 * tt + 128],
                    tb[:, 0:64].bitcast(BF16)),
                    waits=[("pe", P.c("pe")),
                           ("g_swW0", pre["sw2_ready"][0]),
                           ("g_swW1", pre["sw2_ready"][1])], inc="dve")
                setg(gname, "dve", P.c("dve"))
                x2t_evac[0] = P.c("dve")
            x2b_free[0] = P.c("pe")
            st_["x2T_half"][min(1, tt // c.HT)] = P.c("dve")
        st_["x2_ready"] = P.c("dve")
        st_["x2T_ready"] = P.c("dve")

    def emit_sw1_load(gidx):
        par = gidx % 2
        sem = f"g_sw1{par}"
        blk = gidx % (c.F // 512)
        rdy = pre["sw1_blk"].get((512 * blk) // C, pre.get("sw1_ready", (0, 0)))
        waits = [("g_swW0", rdy[0]), ("g_swW1", rdy[1])]
        if st_["sw1_load"][par]:
            waits.append(("pe", st_["sw1_free"][par]))
        src = sw1scr[:, 512 * blk:512 * blk + 512].rearrange(
            "(cc p) co -> p cc co", p=128)
        P.op("sync", lambda e, par=par, src=src: e.dma_start(
            out=sw1b[par][:], in_=src), waits=waits, inc=sem, incn=16)
        st_["sw1_load"][par] = P.c(sem)
        return par

    def emit_sw2_load(gidx):
        par = gidx % 2
        sem = f"g_sw2{par}"
        blk = gidx % (FC // 2)
        rdy = pre["sw2_blk"].get(blk, pre.get("sw2_ready", (0, 0)))
        waits = [("g_swW0", rdy[0]), ("g_swW1", rdy[1])]
        if st_["sw2_load"][par]:
            waits.append(("pe", st_["sw2_free"][par]))
        src = sw2scr[256 * blk:256 * blk + 256, :].rearrange(
            "(hc p) d -> p hc d", p=128)
        P.op("sync", lambda e, par=par, src=src: e.dma_start(
            out=sw2b[par][:], in_=src), waits=waits, inc=sem, incn=16)
        st_["sw2_load"][par] = P.c(sem)
        return par

    sw1_gidx = [0]
    sw2_gidx = [0]
    q1g = []
    q2g = []

    def issue1g():
        q1g.append(emit_sw1_load(sw1_gidx[0]))
        sw1_gidx[0] += 1

    def issue2g():
        q2g.append(emit_sw2_load(sw2_gidx[0]))
        sw2_gidx[0] += 1

    def gen_ffn1(s, ci, half):
        HF = c.HF
        if not q1g:
            issue1g()
        if True:
            toff = half * HF
            for blk in range(c.F // 512):
                while len(q1g) < 2 and blk + 1 < c.F // 512:
                    issue1g()
                par = q1g.pop(0)
                for cl in range(4):
                    co = 4 * blk + cl
                    bank = pS3 if co % 2 == 0 else pS4
                    gname = "pS3" if co % 2 == 0 else "pS4"
                    for k in range(CC):
                        P.op("tensor", lambda e, par=par, cl=cl, k=k,
                             bank=bank, toff=toff: e.matmul(
                                 bank[:, 0:HF],
                                 sw1b[par][:, k, 128 * cl:128 * cl + 128],
                                 x2T[:, k, toff:toff + HF],
                                 start=(k == 0), stop=(k == CC - 1)),
                             waits=([(f"g_sw1{par}", st_["sw1_load"][par]),
                                     ("dve", st_["x2T_half"][half]),
                                     ("act", h_free[co % 2])] + guard(gname)
                                    if k == 0 else []),
                             inc="pe" if k == CC - 1 else None)
                    hm_ = P.c("pe")
                    x2T_rd_free[0] = hm_
                    gw = [("pe", hm_),
                          ("dve", pre["al1_ready"].get(co, 0))]
                    if co >= FC // 2:
                        gw.append(("pe", y2t_free[0]))
                    P.op("scalar", lambda e, co=co, bank=bank: e.activation(
                        hbuf[:, co, :], bank[:, 0:HF], AF.Gelu,
                        bias=b1c[:, co:co + 1], scale=al1c[:, co:co + 1]),
                        waits=gw, inc="act")
                    h_free[co % 2] = P.c("act")
                    setg(gname, "act", P.c("act"))
                    yield
                st_["sw1_free"][par] = P.c("pe")
        st_["h_done"] = P.c("act")

    def gen_ffn2(s, ci, half):
        HF, HT = c.HF, c.HT
        if not q2g:
            issue2g()
        if True:
            for dpass in range(NP2):
                dlo, dhi = dpass * DCP, (dpass + 1) * DCP
                firstw = ([("act", st_["h_done"])] +
                          guard("pF1", "pF2"))
                for blk in range(FC // 2):
                    while len(q2g) < 2 and (blk + 1 < FC // 2 or
                                            dpass + 1 < NP2):
                        issue2g()
                    par = q2g.pop(0)
                    for hcl in range(2):
                        hc = 2 * blk + hcl
                        for dc in range(dlo, dhi):
                            bank, boff, _gn = y2banks[dc]
                            first_in_bank = (boff == 0)
                            last_in_bank = (boff + HF == 512) or (dc == dhi - 1)
                            stt = (hc == 0) and first_in_bank
                            stp = (hc == FC - 1) and last_in_bank
                            w = []
                            if hcl == 0 and dc == dlo:
                                w = [(f"g_sw2{par}", st_["sw2_load"][par])]
                                if blk == 0:
                                    w += firstw
                            P.op("tensor", lambda e, par=par, hcl=hcl, hc=hc,
                                 dc=dc, bank=bank, boff=boff, stt=stt,
                                 stp=stp: e.matmul(
                                     bank[:, boff:boff + HF],
                                     sw2b[par][:, hcl,
                                               128 * dc:128 * dc + 128],
                                     hbuf[:, hc, :], start=stt, stop=stp),
                                 waits=w, inc="pe" if dc == dhi - 1 else None)
                    st_["sw2_free"][par] = P.c("pe")
                    yield
                y2m = P.c("pe")
                for dc in range(dlo, dhi):
                    bank, boff, _gn = y2banks[dc]
                    P.op("scalar", lambda e, dc=dc, bank=bank, boff=boff:
                         e.activation(
                             y2t[:, dc, :], bank[:, boff:boff + HF],
                             AF.Identity, bias=b2c[:, dc:dc + 1],
                             scale=al2c[:, dc:dc + 1]),
                         waits=[("pe", y2m),
                                ("dve", pre.get("al2_ready", 0))], inc="act")
                y2ev = P.c("act")
                for nm in ("pF1", "pF2"):
                    setg(nm, "act", y2ev)
                yield
            for tt in range(HT):
                gtt = half * HT + tt
                for dc in range(CC):
                    tb = pF1 if dc < 4 else pF2
                    gname = "pF1" if dc < 4 else "pF2"
                    P.op("tensor", lambda e, dc=dc, tt=tt, tb=tb: e.transpose(
                        tb[:, 128 * (dc % 4):128 * (dc % 4) + 128],
                        y2t[:, dc, 128 * tt:128 * tt + 128], id32[:]),
                        waits=([("act", y2ev)] + guard("pF1", "pF2")
                               if dc == 0 else []), inc="pe")
                trm = P.c("pe")
                y2t_free[0] = trm
                par = gtt % 2
                ob2 = outb[par]
                ub = ubuf[par]
                uw = [("pe", trm), ("dve", st_["x2_ready"]),
                      ("dve", u_free[par]),
                      ("d_wn3", pre.get("wn_done3", 0) if par else 0),
                      ("d_wn2", pre.get("wn_done2", 0) if par else 0)]
                P.op("vector", lambda e, gtt=gtt, ub=ub: e.tensor_add(
                    ub[:, 0:C], pFW[:, 0:C], x2[:, gtt, 0:C]),
                    waits=uw, inc="dve")
                u2d = P.c("dve")
                setg("pF1", "dve", u2d)
                setg("pF2", "dve", u2d)
                a = emit_ln([ub[:, c.W5 * i:c.W5 * (i + 1)]
                             for i in range(c.NH)],
                            [ob2[:, c.W5 * i:c.W5 * (i + 1)]
                             for i in range(c.NH)],
                            [("dve", u2d),
                             (f"d_st{par}", st_["st_done"][par])], sset=1)
                u_free[par] = a
                tok = ci * TC + gtt * 128
                P.op("sync", lambda e, s=s, tok=tok, ob2=ob2: e.dma_start(
                    out=out_d[s, tok:tok + 128, :], in_=ob2[:, 0:C]),
                    waits=[("dve", a)], inc=f"d_st{par}", incn=16)
                st_["st_done"][par] = P.c(f"d_st{par}")
                yield
        x2_wr_ok[0] = P.c("dve")

    phase = [0]

    def more():
        phase[0] += 1
        return phase[0] <= TRUNC

    def drain(g, n=10 ** 9):
        k = 0
        while k < n:
            try:
                next(g)
            except StopIteration:
                return None
            k += 1
        return g

    emit_preamble()

    fills = []

    def fill(k):
        while k > 0 and fills:
            try:
                next(fills[0])
                k -= 1
            except StopIteration:
                fills.pop(0)

    def drain_all(g):
        if g in fills:
            for _ in g:
                pass
            fills.remove(g)

    pre_gen = gen_preambleB()
    fills.append(pre_gen)
    prev_chunk = None

    def ffn_stage(half):
        if prev_chunk is None:
            return
        for _ in gen_ffn1(prev_chunk[0], prev_chunk[1], half):
            pass
        fills.append(gen_ffn2(prev_chunk[0], prev_chunk[1], half))

    for s in range(c.S):
        if phase[0] > TRUNC:
            break
        P.op("vector", lambda e: e.memset(khat[:], 0.0),
             waits=[("dve", st_["negm_ready"])], inc="dve")
        vv = Vt[:].rearrange("p (g h v) -> p g h v", g=c.GT, h=H)
        P.op("vector", lambda e, vv=vv: e.memset(vv[:, :, :, 64:65], 1.0),
             waits=[("pe", P.c("pe")), ("dve", ot_evac[0]),
                    ("dve", praw_free[0])], inc="dve")
        ones_done = P.c("dve")
        for ci in range(c.NCH):
            if phase[0] > TRUNC:
                break
            for half in range(2):
                if more():
                    if s == 0 and ci == 0:
                        emit_xt_load(s, ci, half)
                    emit_latent_v(s, ci, half)
                    fill(4)
            if more():
                emit_negm()
            if ci == 0:
                P.op("tensor", lambda e: e.nop(), waits=[("dve", ones_done)])
            for tt in range(min(2, TT)):
                emit_xres_load(s, ci, tt)
            n_attn = max(1, (c.PAIRS // 2) * (TT * (ci + 1) + 2))
            n_ffn = (FC // 2) + 1 + c.HT + 3
            n_pre = FC + CC * (c.F // C)
            est = (n_ffn if any(g is not pre_gen for g in fills) else 0) + \
                  (n_pre // 2 if pre_gen in fills else 0)
            ratio = est / n_attn
            acc = [0.0]
            pf_done = [False]
            for p in range(c.PAIRS):
                if not more():
                    break
                for _ in gen_attn_pair(s, ci, p):
                    acc[0] += ratio
                    k = int(acc[0])
                    if k > 0:
                        acc[0] -= k
                        fill(k)
                if p == c.PAIRS // 2 - 1 and not pf_done[0]:
                    pf_done[0] = True
                    if ci + 1 < c.NCH:
                        emit_xt_load(s, ci + 1, 0)
                        emit_xt_load(s, ci + 1, 1)
                    elif s + 1 < c.S:
                        emit_xt_load(s + 1, 0, 0)
                        emit_xt_load(s + 1, 0, 1)
                    for g in list(fills):
                        if g is not pre_gen:
                            drain_all(g)
                    ffn_stage(1)
                    if prev_chunk is not None and not q1g:
                        issue1g()
            for g in list(fills):
                drain_all(g)
            if more():
                emit_oproj_ln1(s, ci, range(0, c.HT))
                prev_chunk = (s, ci)
                ffn_stage(0)
                emit_oproj_ln1(s, ci, range(c.HT, TT))
    for g in list(fills):
        drain_all(g)
    ffn_stage(1)
    for g in list(fills):
        drain_all(g)

    P.op("sync", lambda e: e.nop(),
         waits=[("d_st0", st_["st_done"][0]), ("d_st1", st_["st_done"][1])])

    with es:
        with nc.Block() as block:
            @block.sync
            def _(e):
                P.run("sync", e, sems)

            @block.gpsimd
            def _(e):
                P.run("gpsimd", e, sems)

            @block.tensor
            def _(e):
                P.run("tensor", e, sems)

            @block.vector
            def _(e):
                P.run("vector", e, sems)

            @block.scalar
            def _(e):
                P.run("scalar", e, sems)

    return nc


def _prep_core_inputs(cfg, hs, bw, core, hr, vw, vb, ow, ob, w1, b1, w2, b2):
    c = cfg
    cn = np.ascontiguousarray
    f = np.float32
    return {
        "hsT": cn(hs.transpose(0, 2, 1)).astype(f),
        "hs": cn(hs).astype(f),
        "bwT": cn(bw.T).astype(f),
        "coreA": cn(core).astype(f),
        "coreB": cn(core.T).astype(f),
        "hrr": cn(hr.transpose(1, 0, 2).reshape(c.R, c.H * c.R)).astype(f),
        "vwT": cn(vw.T).astype(f),
        "owT": cn(ow.T).astype(f),
        "vbr": cn(np.asarray(vb).reshape(1, c.C)).astype(f),
        "obr": cn(np.asarray(ob).reshape(1, c.C)).astype(f),
        "w1n": cn(w1).astype(f),
        "w1t": cn(w1.T).astype(f),
        "b1c": cn(np.asarray(b1).reshape(c.FC, 128).T).astype(f),
        "w2n": cn(w2).astype(f),
        "w2t": cn(w2.T).astype(f),
        "b2c": cn(np.asarray(b2).reshape(c.CC, 128).T).astype(f),
    }


def kernel(L0_attn_hidden_states, L0_attn_basis_w, L0_attn_core,
           L0_attn_head_residual, L0_attn_v_w, L0_attn_v_b, L0_attn_o_w,
           L0_attn_o_b, L0_ffn_w1, L0_ffn_b1, L0_ffn_w2, L0_ffn_b2):
    cfg = CFG()
    n = 8
    hs = np.asarray(L0_attn_hidden_states, dtype=np.float32)
    args = [np.asarray(a, dtype=np.float32) for a in
            (L0_attn_basis_w, L0_attn_core, L0_attn_head_residual,
             L0_attn_v_w, L0_attn_v_b, L0_attn_o_w, L0_attn_o_b,
             L0_ffn_w1, L0_ffn_b1, L0_ffn_w2, L0_ffn_b2)]
    nc = build_program(cfg)
    in_maps = [_prep_core_inputs(cfg, hs[cfg.S * i:cfg.S * (i + 1)], *args)
               for i in range(n)]
    res = run_bass_kernel_spmd(nc, in_maps, core_ids=list(range(n)))
    return np.concatenate([r["out"] for r in res.results], axis=0)
